# version 15
# baseline (speedup 1.0000x reference)
import math
from contextlib import ExitStack

import numpy as np

import concourse.bass as bass
import concourse.mybir as mybir
from concourse.bass_utils import run_bass_kernel_spmd

F32 = mybir.dt.float32
BF16 = mybir.dt.bfloat16
AF = mybir.ActivationFunctionType
ALU = mybir.AluOpType
AX = mybir.AxisListType

B, S, D, DQK, DV, FF, E = 2, 8192, 1024, 128, 1024, 1408, 8
DCH, JCH, NT, TPC = 8, 11, 16, 2048
QT_OFF = [0, 384, 768]
QT_W = [384, 384, 256]
DIAG_SUBS = [3, 6, 8]
EPS = 1e-5
MU = math.sqrt(0.5)
SIGSQ2 = math.sqrt(0.25 * math.pi) * math.sqrt(2.0)
CST = 1.0 / (S * SIGSQ2)
ZS = math.sqrt(128.0) / SIGSQ2
ZSHIFT = -MU / SIGSQ2
NEG = -30000.0
ZEXT = 8832
ZLEN = 8704

_LEVEL = 7
_PARANOID = 0
N_DIAG_SUB = 2 * sum(DIAG_SUBS)
N_SUB = N_DIAG_SUB + 7 * 24
N_QT = 27
_DELAY_EVAC = True
N_EV = 72
_NQS_SEQ = [3, 3, 2] * 9
_EVCUM = [0]
for _n in _NQS_SEQ:
    _EVCUM.append(_EVCUM[-1] + _n)


def _bucket_table():
    NUM_BUCKETS, MAX_DISTANCE = 32, 128
    n = np.arange(S, dtype=np.int64)
    max_exact = NUM_BUCKETS // 2
    nf = np.maximum(n, 1).astype(np.float32)
    val_if_large = max_exact + (
        np.log(nf / max_exact) / math.log(MAX_DISTANCE / max_exact) * (NUM_BUCKETS - max_exact)
    ).astype(np.int32)
    val_if_large = np.minimum(val_if_large, NUM_BUCKETS - 1)
    return np.where(n < max_exact, n, val_if_large).astype(np.int64)


def _host_consts():
    import ml_dtypes
    cmat = np.zeros((33, ZEXT), np.float32)
    buck = _bucket_table()
    d = np.arange(ZEXT) - 640
    valid = (d >= 0) & (d < S)
    cmat[32, :] = np.float32(NEG / ZSHIFT)
    cmat[32, valid] = 1.0
    cols = np.nonzero(valid)[0]
    cmat[buck[d[cols]], cols] = ZS
    ident = np.eye(128).astype(ml_dtypes.bfloat16)
    return cmat, ident


def _owner(g):
    return (g, 0) if g < 4 else (7 - g, 1)


class Trk:
    def __init__(self, ctx, nc, name):
        self.h = ctx.enter_context(nc.semaphore(name))
        self.n = 0

    def inc(self, inst, k=1):
        inst.then_inc(self.h, k)
        self.n += k
        return self.n


def _attn_plan(r):
    plan = []
    gsub, qtg = 0, 0
    for slot in (0, 1):
        for qt in range(3):
            w, ns = QT_W[qt], DIAG_SUBS[qt]
            for sub in range(ns):
                plan.append(dict(
                    gsub=gsub, qtg=qtg, w=w, nqs=w // 128,
                    first=(sub == 0), last=(sub == ns - 1),
                    slot=slot, qoff=QT_OFF[qt], kind="diag",
                    kloc=slot * 1024 + sub * 128,
                    zcol=QT_OFF[qt] - sub * 128 + 512,
                    vrow=slot * 1024 + sub * 128, vdiag=True))
                gsub += 1
            qtg += 1
    if r is None:
        return plan
    uo = 0
    for slot, blk in ((0, r), (1, 7 - r)):
        for kb8 in range(blk):
            for qt in range(3):
                w = QT_W[qt]
                for sub in range(8):
                    plan.append(dict(
                        gsub=gsub, qtg=qtg, w=w, nqs=w // 128,
                        first=(sub == 0), last=(sub == 7),
                        qoff=QT_OFF[qt], kind="off", uo=uo, sub=sub, qt=qt))
                    gsub += 1
                qtg += 1
            uo += 1
    assert gsub == N_SUB and qtg == N_QT and uo == 7
    return plan


ZROW = 9216

RING_SLOTS = 16


def _ffn_sched():
    free_of = {}
    lookup = {}
    ordinal = [0] * RING_SLOTS
    slot_vt = [(-1.0, None)] * RING_SLOTS
    for call in (0, 1):
        for e in range(E):
            seq = [("w1", k) for k in range(4)] + [("w2", j) for j in range(JCH)] + \
                  [("w1", k) for k in range(4, 22)]
            for kind, idx in seq:
                s = min(range(RING_SLOTS), key=lambda i: slot_vt[i][0])
                vt_old, ev = slot_vt[s]
                ordinal[s] += 1
                lookup[(kind, call, e, idx)] = (s, ordinal[s], ev)
                if kind == "w1":
                    uw_last = call * 352 + e * 44 + (idx // 2) * 4 + 3
                    slot_vt[s] = (float(uw_last), ("w1", uw_last + 1, idx % 2))
                else:
                    ce = call * 8 + e
                    slot_vt[s] = (call * 352.0 + (e + 1) * 44 - 0.01, ("w2", ce + 1))
    return lookup


_FFN_SCHED = _ffn_sched()



def _attn_tables(r):
    p = np.arange(128)
    itab = np.zeros((128, 203), np.int32)
    ftab = np.zeros((128, 14), np.float32)
    units = []
    for slot, blk in ((0, r), (1, 7 - r)):
        for kb8 in range(blk):
            units.append((slot, blk, kb8))
    assert len(units) == 7
    vcol = 14
    for uo, (slot, blk, kb8) in enumerate(units):
        ro, so = _owner(kb8)
        itab[:, uo] = (ro * 128 + p) * 2 + so
        itab[:, 7 + uo] = 2 * p + slot
        dk = blk - kb8
        for j in range(3):
            itab[:, 182 + uo * 3 + j] = 9 * p + (dk - 1 + j)
        ftab[:, uo] = 1.0 if slot == 1 else 0.0
        ftab[:, 7 + uo] = 1.0 - ftab[:, uo]
        for qt in range(3):
            for sub in range(8):
                itab[:, vcol] = (ro * 2048 + so * 1024 + sub * 128) + p
                vcol += 1
    assert vcol == 182
    return itab, ftab


def build(level=7, paranoid=0):
    nc = bass.Bass("TRN2", target_bir_lowering=False)

    x0 = nc.dram_tensor("x0", [TPC, D], F32, kind="ExternalInput").ap()
    wqk_d = nc.dram_tensor("wqk", [D, DQK], F32, kind="ExternalInput").ap()
    bqk_d = nc.dram_tensor("bqk", [DQK, 1], F32, kind="ExternalInput").ap()
    gam_d = nc.dram_tensor("gam", [2, DQK], F32, kind="ExternalInput").ap()
    bet_d = nc.dram_tensor("bet", [2, DQK], F32, kind="ExternalInput").ap()
    wv_d = nc.dram_tensor("wv", [D, DV], F32, kind="ExternalInput").ap()
    bv_d = nc.dram_tensor("bv", [1, DV], F32, kind="ExternalInput").ap()
    rel_d = nc.dram_tensor("rel", [32, 1], F32, kind="ExternalInput").ap()
    gw_d = nc.dram_tensor("gw", [E, D], F32, kind="ExternalInput").ap()
    w1_d = nc.dram_tensor("w1", [E, D, 2 * FF], F32, kind="ExternalInput").ap()
    w2_d = nc.dram_tensor("w2", [E, FF, D], F32, kind="ExternalInput").ap()
    cmat_d = nc.dram_tensor("cmat", [33, ZEXT], F32, kind="ExternalInput").ap()
    ident_d = nc.dram_tensor("ident", [128, 128], BF16, kind="ExternalInput").ap()
    itab_d = nc.dram_tensor("itab", [128, 203], mybir.dt.int32, kind="ExternalInput").ap()
    ftab_d = nc.dram_tensor("ftab", [128, 14], F32, kind="ExternalInput").ap()
    out_d = nc.dram_tensor("out", [TPC, D], F32, kind="ExternalOutput").ap()

    zext_dram = nc.dram_tensor("zext_dram", [1, ZEXT], F32)
    kin_dram = nc.dram_tensor("kin_dram", [DQK, TPC], BF16)
    kall_dram = nc.dram_tensor("kall_dram", [4 * DQK, TPC], BF16)
    vin_dram = nc.dram_tensor("vin_dram", [TPC, DV], BF16)
    vall_dram = nc.dram_tensor("vall_dram", [4 * TPC, DV], BF16)
    q_dram = nc.dram_tensor("q_dram", [DQK, TPC], BF16)
    z_dram = nc.dram_tensor("z_dram", [128, ZROW], F32)

    TT, VV, SS, GG, SY = nc.tensor, nc.vector, nc.scalar, nc.gpsimd, nc.sync

    ctx = ExitStack()
    with ctx:
        sb = lambda n_, s_, d_=F32: ctx.enter_context(nc.sbuf_tensor(n_, s_, d_))
        x_tm = sb("x_tm", [128, NT * D])
        xn_fm = sb("xn_fm", [128, DCH * TPC], BF16)
        xn_rot = sb("xn_rot", [128, 2 * D], BF16)
        comb = sb("comb", [128, NT * E])
        lg_all = sb("lg_all", [128, NT * E])
        gex_all = sb("gex_all", [128, NT * E])
        gneg = sb("gneg", [128, NT])
        g_scr = sb("g_scr", [128, 4 * E + 8])
        lnS = sb("lnS", [128, NT])
        lnQ = sb("lnQ", [128, NT])
        lnM = sb("lnM", [128, NT])
        lnV = sb("lnV", [128, NT])
        lnR = sb("lnR", [128, NT])
        lnI = sb("lnI", [128, NT])
        sq_scr = sb("sq_scr", [128, D])
        bqk_c = sb("bqk_c", [128, 1])
        gam_raw = sb("gam_raw", [128, 4])
        gamv = sb("gamv", [128, 4])
        bv_rep = sb("bv_rep", [128, DV])
        gwT = sb("gwT", [128, DCH * E], BF16)
        wqk_sb = sb("wqk_sb", [128, DCH * DQK], BF16)
        g8 = sb("g8", [8, D], BF16)
        zl_sb = sb("zl_sb", [33, 1])
        ident_sb = sb("ident_sb", [128, 128], BF16)
        itab_sb = sb("itab_sb", [128, 203], mybir.dt.int32)
        ftab_sb = sb("ftab_sb", [128, 14])

        s_sy = Trk(ctx, nc, "s_sy")
        s_gq = Trk(ctx, nc, "s_gq")
        s_w1l = Trk(ctx, nc, "s_w1l")
        s_w2l = Trk(ctx, nc, "s_w2l")
        s_scq = Trk(ctx, nc, "s_scq")
        t_tp = Trk(ctx, nc, "t_tp")
        t_te = Trk(ctx, nc, "t_te")
        c_lnv = Trk(ctx, nc, "c_lnv")
        c_lns = Trk(ctx, nc, "c_lns")
        c_gt = Trk(ctx, nc, "c_gt")
        c_gv = Trk(ctx, nc, "c_gv")
        c_gs = Trk(ctx, nc, "c_gs")
        f_g = Trk(ctx, nc, "f_g")
        f_u = Trk(ctx, nc, "f_u")
        f_sg = Trk(ctx, nc, "f_sg")
        f_h = Trk(ctx, nc, "f_h")
        f_ym = Trk(ctx, nc, "f_ym")
        f_ye = Trk(ctx, nc, "f_ye")
        p_t = Trk(ctx, nc, "p_t")
        p_s = Trk(ctx, nc, "p_s")
        p_v = Trk(ctx, nc, "p_v")
        a_cc = Trk(ctx, nc, "a_cc")
        a_sim = Trk(ctx, nc, "a_sim")
        a_arg = Trk(ctx, nc, "a_arg")
        a_erf = Trk(ctx, nc, "a_erf")
        a_at = Trk(ctx, nc, "a_at")
        a_av = Trk(ctx, nc, "a_av")
        a_ev = Trk(ctx, nc, "a_ev")
        a_vl = Trk(ctx, nc, "a_vl")
        s_zb = Trk(ctx, nc, "s_zb")
        skq2 = [Trk(ctx, nc, "skq0"), Trk(ctx, nc, "skq1")]
        szu2 = [Trk(ctx, nc, "szu0"), Trk(ctx, nc, "szu1")]
        avl3 = [Trk(ctx, nc, "avl%d" % _a) for _a in range(5)]
        scq2 = [Trk(ctx, nc, "scq0a"), Trk(ctx, nc, "scq1a")]
        w1s = [Trk(ctx, nc, f"w1s{i}") for i in range(RING_SLOTS)]
        z_ev = Trk(ctx, nc, "z_ev")
        v_sf = Trk(ctx, nc, "v_sf")
        s_sf = Trk(ctx, nc, "s_sf")
        s_out = Trk(ctx, nc, "s_out")

        block = ctx.enter_context(nc.Block())
        CK = {}

        def vs(inst):
            v_sf.inc(inst)
            VV.wait_ge(v_sf.h, v_sf.n)

        def xt(i):
            return x_tm[:, i * D:(i + 1) * D]

        @block.gpsimd
        def _(g):
            s_gq.inc(g.dma_start(out=wqk_sb[:].rearrange("p (d c) -> p d c", c=DQK),
                                 in_=wqk_d.rearrange("(d p) c -> p d c", p=128)), 16)
            s_gq.inc(g.dma_start(out=g8[:], in_=gw_d), 16)
        CK["gq_const"] = s_gq.n

        @block.sync
        def _(sy):
            s_sy.inc(sy.dma_start(out=zl_sb[0:32, :], in_=rel_d), 16)
            CK["rel"] = s_sy.n
            s_sy.inc(sy.dma_start(out=x_tm[:].rearrange("p (t m) -> p t m", m=D),
                                  in_=x0.rearrange("(t p) m -> p t m", p=128)), 16)
            CK["x0"] = s_sy.n
            s_sy.inc(sy.dma_start(out=bqk_c[:], in_=bqk_d), 16)
            s_sy.inc(sy.dma_start(out=gam_raw[:, 0:1], in_=gam_d[0:1, :].rearrange("o p -> p o")), 16)
            s_sy.inc(sy.dma_start(out=gam_raw[:, 1:2], in_=bet_d[0:1, :].rearrange("o p -> p o")), 16)
            s_sy.inc(sy.dma_start(out=gam_raw[:, 2:3], in_=gam_d[1:2, :].rearrange("o p -> p o")), 16)
            s_sy.inc(sy.dma_start(out=gam_raw[:, 3:4], in_=bet_d[1:2, :].rearrange("o p -> p o")), 16)
            s_sy.inc(sy.dma_start(out=bv_rep[:], in_=bass.AP(bv_d.tensor, 0, [[0, 128], [1, DV]])), 16)
            s_sy.inc(sy.dma_start(out=itab_sb[:], in_=itab_d), 16)
            s_sy.inc(sy.dma_start(out=ftab_sb[:], in_=ftab_d), 16)
            CK["smalls"] = s_sy.n
            CK["atab"] = s_sy.n

        init_scope = ExitStack()
        zps = init_scope.enter_context(nc.psum_tensor("zps", [128, 1024], F32))
        gtp = init_scope.enter_context(nc.psum_tensor("gtp", [128, DCH * E], BF16))
        cmat_sb = init_scope.enter_context(nc.sbuf_tensor("cmat_sb", [33, ZEXT], F32))
        zext_sb = init_scope.enter_context(nc.sbuf_tensor("zext_sb", [1, ZEXT], F32))

        @block.sync
        def _(sy):
            s_sy.inc(sy.dma_start(out=cmat_sb[:], in_=cmat_d), 16)
            s_sy.inc(sy.dma_start(out=ident_sb[:], in_=ident_d), 16)
        CK["cmat"] = s_sy.n

        @block.vector
        def _(v):
            v.wait_ge(s_sy.h, CK["smalls"])
            s_zb.inc(v.memset(zl_sb[32:33, :], ZSHIFT))
        CK["zl"] = s_zb.n

        zchunks = []
        c0 = 0
        while c0 < ZEXT:
            zchunks.append((c0, min(512, ZEXT - c0)))
            c0 += 512

        @block.tensor
        def _(t):
            t.wait_ge(s_sy.h, CK["cmat"])
            t.wait_ge(s_zb.h, CK["zl"])
            for i, (c, w) in enumerate(zchunks):
                if i >= 2:
                    t.wait_ge(z_ev.h, i - 1)
                s_zb.inc(t.matmul(zps[0:1, (i % 2) * 512:(i % 2) * 512 + w],
                                  zl_sb[:, :], cmat_sb[:, c:c + w], start=True, stop=True))
            t.wait_ge(s_gq.h, CK["gq_const"])
            for dd in range(DCH):
                s_zb.inc(t.transpose(gtp[:, dd * E:(dd + 1) * E],
                                     g8[0:8, dd * 128:(dd + 1) * 128], ident_sb[0:8, 0:8]))
        CK["gtp"] = s_zb.n

        @block.vector
        def _(v):
            for i, (c, w) in enumerate(zchunks):
                v.wait_ge(s_zb.h, CK["zl"] + i + 1)
                z_ev.inc(v.tensor_copy(zext_sb[0:1, c:c + w], zps[0:1, (i % 2) * 512:(i % 2) * 512 + w]))
        CK["zev"] = z_ev.n

        @block.scalar
        def _(s):
            s.wait_ge(s_zb.h, CK["gtp"])
            s_sf.inc(s.copy(gwT[:], gtp[:]))
            s.wait_ge(s_sf.h, s_sf.n)
        CK["gwT"] = s_sf.n

        zpad_sb = init_scope.enter_context(nc.sbuf_tensor("zpad_sb", [128, ZROW - ZLEN], F32))

        @block.vector
        def _(v):
            s_zb.inc(v.memset(zpad_sb[:], 0.0))
        CK["zpad"] = s_zb.n

        @block.sync
        def _(sy):
            sy.wait_ge(z_ev.h, CK["zev"])
            s_sy.inc(sy.dma_start(out=zext_dram.ap(), in_=zext_sb[:]), 16)
            CK["zext_store"] = s_sy.n
            sy.wait_ge(s_sy.h, CK["zext_store"])
            for p in range(128):
                s_zb.inc(sy.dma_start(
                    out=bass.AP(z_dram, p * ZROW, [[1, 1], [1, ZLEN]]),
                    in_=bass.AP(zext_dram, 128 - p, [[1, 1], [1, ZLEN]])), 16)
            sy.wait_ge(s_zb.h, CK["zpad"])
            s_zb.inc(sy.dma_start(
                out=bass.AP(z_dram, ZLEN, [[ZROW, 128], [1, ZROW - ZLEN]]),
                in_=zpad_sb[:]), 16)
        CK["zdram"] = s_zb.n

        init_scope.close()

        def ln_phase(L, tp_ps):
            te0, tp0 = t_te.n, t_tp.n
            for i in range(NT):
                CK[f"tr{L}_{i}"] = te0 + i + 1

            @block.vector
            def _(v):
                if L == 0:
                    v.wait_ge(s_sy.h, CK["x0"])
                for i in range(NT):
                    v.reduce_sum(lnS[:, i:i + 1], xt(i), axis=AX.X)
                    inst = v.scalar_tensor_tensor(sq_scr[:], xt(i), 1.0, xt(i),
                                                  ALU.mult, ALU.mult, accum_out=lnQ[:, i:i + 1])
                    if i == NT - 1:
                        vs(inst)
                v.tensor_scalar_mul(lnM[:], lnS[:], 1.0 / D)
                vs(v.tensor_scalar_mul(lnQ[:], lnQ[:], 1.0 / D))
                vs(v.tensor_tensor(lnV[:], lnM[:], lnM[:], ALU.mult))
                vs(v.tensor_tensor(lnV[:], lnV[:], lnQ[:], ALU.subtract))
                vs(v.tensor_scalar(lnV[:], lnV[:], -1.0, EPS, ALU.mult, ALU.add))
                c_lnv.inc(v.reciprocal(lnR[:], lnV[:]))
                CK[f"ln{L}_rcp"] = c_lnv.n
                v.wait_ge(c_lns.h, L + 1)
                for i in range(NT):
                    if i >= 2:
                        v.wait_ge(t_tp.h, tp0 + i - 1)
                    c_lnv.inc(v.tensor_scalar(xn_rot[:, (i % 2) * D:(i % 2 + 1) * D],
                                              xt(i), lnM[:, i:i + 1], lnI[:, i:i + 1],
                                              ALU.subtract, ALU.mult))
                    CK[f"ln{L}_n{i}"] = c_lnv.n

            @block.tensor
            def _(t):
                if L == 0:
                    t.wait_ge(z_ev.h, CK["zev"])
                    t.wait_ge(s_sf.h, CK["gwT"])
                elif L == 1:
                    t.wait_ge(f_ye.h, 128)
                else:
                    t.wait_ge(a_ev.h, N_EV)
                for i in range(NT):
                    t.wait_ge(c_lnv.h, CK[f"ln{L}_n{i}"])
                    if i >= 2:
                        t.wait_ge(t_te.h, te0 + i - 1)
                    for dd in range(DCH):
                        inst = t.transpose(tp_ps[i % 2][:, dd * 128:(dd + 1) * 128],
                                           xn_rot[:, (i % 2) * D + dd * 128:(i % 2) * D + (dd + 1) * 128],
                                           ident_sb[:])
                    t_tp.inc(inst)

            @block.scalar
            def _(s):
                s.wait_ge(c_lnv.h, CK[f"ln{L}_rcp"])
                c_lns.inc(s.activation(lnI[:], lnR[:], AF.Sqrt))
                for i in range(NT):
                    s.wait_ge(t_tp.h, tp0 + i + 1)
                    for dd in range(DCH):
                        inst = s.copy(xn_fm[:, dd * TPC + i * 128:dd * TPC + (i + 1) * 128],
                                      tp_ps[i % 2][:, dd * 128:(dd + 1) * 128])
                    t_te.inc(inst)

        def tr_wait(eng, L, last_tile):
            eng.wait_ge(t_te.h, CK[f"tr{L}_{last_tile}"])


        def gating_phase(call, gpsAB):
            L = 0 if call == 0 else 2
            gt0, gv0, gs0 = c_gt.n, c_gv.n, c_gs.n

            def gps(i):
                return gpsAB[i % 2][:, 0:8]

            @block.tensor
            def _(t):
                if call == 0:
                    t.wait_ge(s_sf.h, CK["gwT"])
                    t.wait_ge(z_ev.h, CK["zev"])
                else:
                    t.wait_ge(a_ev.h, N_EV)
                for i in range(NT):
                    if i >= 2:
                        t.wait_ge(c_gv.h, gv0 + i - 1)
                    tr_wait(t, L, i)
                    for dd in range(DCH):
                        inst = t.matmul(gps(i),
                                        xn_fm[:, dd * TPC + i * 128:dd * TPC + (i + 1) * 128],
                                        gwT[:, dd * E:(dd + 1) * E],
                                        start=(dd == 0), stop=(dd == DCH - 1))
                    c_gt.inc(inst)

            @block.vector
            def _(v):
                for i in range(NT):
                    v.wait_ge(c_gt.h, gt0 + i + 1)
                    v.reduce_max(gneg[:, i:i + 1], gps(i), axis=AX.X, negate=True)
                    c_gv.inc(v.tensor_copy(lg_all[:, i * 8:(i + 1) * 8], gps(i)))
                CK[f"gatecopies{call}"] = c_gv.n

            @block.scalar
            def _(s):
                for i in range(NT):
                    s.wait_ge(c_gv.h, gv0 + i + 1)
                    c_gs.inc(s.activation(gex_all[:, i * 8:(i + 1) * 8], lg_all[:, i * 8:(i + 1) * 8],
                                          AF.Exp, bias=gneg[:, i:i + 1], scale=1.0))

            eq1 = g_scr[:, 0:E]
            sc2 = g_scr[:, E:2 * E]
            eq2 = g_scr[:, 2 * E:3 * E]
            gtmp = g_scr[:, 3 * E:4 * E]
            t1 = g_scr[:, 4 * E + 0:4 * E + 1]
            t2 = g_scr[:, 4 * E + 1:4 * E + 2]
            sm = g_scr[:, 4 * E + 2:4 * E + 3]
            dn = g_scr[:, 4 * E + 3:4 * E + 4]
            rd = g_scr[:, 4 * E + 4:4 * E + 5]
            cc1 = g_scr[:, 4 * E + 5:4 * E + 6]
            cc2 = g_scr[:, 4 * E + 6:4 * E + 7]

            @block.vector
            def _(v):
                for i in range(NT):
                    ex = gex_all[:, i * 8:(i + 1) * 8]
                    v.wait_ge(c_gs.h, gs0 + i + 1)
                    v.reduce_max(t1, ex, axis=AX.X)
                    vs(v.reduce_sum(sm, ex, axis=AX.X))
                    vs(v.tensor_scalar(eq1, ex, t1, None, ALU.is_ge))
                    vs(v.scalar_tensor_tensor(sc2, eq1, -1e30, ex, ALU.mult, ALU.add))
                    vs(v.reduce_max(t2, sc2, axis=AX.X))
                    v.tensor_scalar(eq2, sc2, t2, None, ALU.is_ge)
                    vs(v.tensor_tensor(dn, t1, t2, ALU.add))
                    vs(v.scalar_tensor_tensor(dn, sm, 1e-20, dn, ALU.mult, ALU.add))
                    vs(v.reciprocal(rd, dn))
                    v.tensor_scalar(cc1, t1, rd, 0.5, ALU.mult, ALU.mult)
                    vs(v.tensor_scalar(cc2, t2, rd, 0.5, ALU.mult, ALU.mult))
                    vs(v.tensor_scalar_mul(gtmp, eq1, cc1))
                    inst = v.scalar_tensor_tensor(comb[:, i * 8:(i + 1) * 8], eq2, cc2, gtmp,
                                                  ALU.mult, ALU.add)
                    if i == NT - 1:
                        c_gv.inc(inst)
            CK[f"gate{call}"] = c_gv.n

        def ffn_phase(call, w1ring, psA, psB, psY, silu_buf, hsc):
            L = 0 if call == 0 else 2

            @block.tensor
            def _(t):
                t.wait_ge(c_gv.h, CK[f"gatecopies{call}"])
                tr_wait(t, L, NT - 1)
                for e in range(E):
                    for j in range(JCH):
                        kg, ku = 2 * j, 2 * j + 1
                        sg, og, _ = _FFN_SCHED[("w1", call, e, kg)]
                        su, ou, _ = _FFN_SCHED[("w1", call, e, ku)]
                        for tt in range(4):
                            uw = call * 352 + e * 44 + j * 4 + tt
                            if uw >= 2:
                                t.wait_ge(f_sg.h, uw - 1)
                            t.wait_ge(w1s[sg].h, 16 * og)
                            tr_wait(t, L, 4 * tt + 3)
                            for dd in range(DCH):
                                inst = t.matmul(
                                    psA[:, (uw % 2) * 512:(uw % 2) * 512 + 512],
                                    w1ring[:, sg * 1024 + dd * 128:sg * 1024 + (dd + 1) * 128],
                                    xn_fm[:, dd * TPC + tt * 512:dd * TPC + (tt + 1) * 512],
                                    start=(dd == 0), stop=(dd == DCH - 1))
                            f_g.inc(inst)
                            if uw >= 2:
                                t.wait_ge(f_h.h, uw - 1)
                            t.wait_ge(w1s[su].h, 16 * ou)
                            for dd in range(DCH):
                                inst = t.matmul(
                                    psB[:, (uw % 2) * 512:(uw % 2) * 512 + 512],
                                    w1ring[:, su * 1024 + dd * 128:su * 1024 + (dd + 1) * 128],
                                    xn_fm[:, dd * TPC + tt * 512:dd * TPC + (tt + 1) * 512],
                                    start=(dd == 0), stop=(dd == DCH - 1))
                            f_u.inc(inst)
                    t.wait_ge(f_h.h, call * 352 + (e + 1) * 44)
                    w2slot = {}
                    for j in range(JCH):
                        s2, o2, _ = _FFN_SCHED[("w2", call, e, j)]
                        w2slot[j] = s2
                        t.wait_ge(w1s[s2].h, 16 * o2)
                    for tt16 in range(NT):
                        uy = call * 128 + e * 16 + tt16
                        if uy >= 2:
                            t.wait_ge(f_ye.h, uy - 1)
                        for j in range(JCH):
                            for h in range(2):
                                inst = t.matmul(
                                    psY[uy % 2][:, h * 512:(h + 1) * 512],
                                    hsc[:, j * TPC + tt16 * 128:j * TPC + (tt16 + 1) * 128],
                                    w1ring[:, w2slot[j] * 1024 + h * 512:w2slot[j] * 1024 + (h + 1) * 512],
                                    start=(j == 0), stop=(j == JCH - 1))
                        f_ym.inc(inst)

            @block.scalar
            def _(s):
                for e in range(E):
                    for j in range(JCH):
                        for tt in range(4):
                            uw = call * 352 + e * 44 + j * 4 + tt
                            s.wait_ge(f_g.h, uw + 1)
                            if uw >= 2:
                                s.wait_ge(f_h.h, uw - 1)
                            f_sg.inc(s.activation(silu_buf[:, (uw % 2) * 512:(uw % 2) * 512 + 512],
                                                  psA[:, (uw % 2) * 512:(uw % 2) * 512 + 512], AF.Silu))

            @block.vector
            def _(v):
                for e in range(E):
                    for j in range(JCH):
                        for tt in range(4):
                            uw = call * 352 + e * 44 + j * 4 + tt
                            v.wait_ge(f_sg.h, uw + 1)
                            v.wait_ge(f_u.h, uw + 1)
                            f_h.inc(v.tensor_tensor(
                                hsc[:, j * TPC + tt * 512:j * TPC + (tt + 1) * 512],
                                silu_buf[:, (uw % 2) * 512:(uw % 2) * 512 + 512],
                                psB[:, (uw % 2) * 512:(uw % 2) * 512 + 512], ALU.mult))
                    for tt16 in range(NT):
                        uy = call * 128 + e * 16 + tt16
                        v.wait_ge(f_ym.h, uy + 1)
                        f_ye.inc(v.scalar_tensor_tensor(
                            xt(tt16), psY[uy % 2][:],
                            comb[:, tt16 * 8 + e:tt16 * 8 + e + 1],
                            xt(tt16), ALU.mult, ALU.add))

        def gp_ffn_weights(g, call, ring, guard=None):
            if guard is not None:
                g.wait_ge(*guard)
            for e in range(E):
                seq = [("w1", k) for k in range(4)] + [("w2", j) for j in range(JCH)] + \
                      [("w1", k) for k in range(4, 22)]
                for kind, idx in seq:
                    slot, ordn, ev = _FFN_SCHED[(kind, call, e, idx)]
                    if ev is not None:
                        if ev[0] == "w1":
                            g.wait_ge((f_g if ev[2] == 0 else f_u).h, ev[1])
                        else:
                            g.wait_ge(f_ym.h, 16 * ev[1])
                    if kind == "w1":
                        col = (idx // 2) + JCH * (idx % 2)
                        w1s[slot].inc(g.dma_start(
                            out=bass.AP(ring, slot * 1024, [[RING_SLOTS * 1024, 128], [128, DCH], [1, 128]]),
                            in_=bass.AP(w1_d.tensor, e * D * 2 * FF + col * 128,
                                        [[2 * FF, 128], [128 * 2 * FF, DCH], [1, 128]])), 16)
                    else:
                        w1s[slot].inc(g.dma_start(
                            out=bass.AP(ring, slot * 1024, [[RING_SLOTS * 1024, 128], [1, 1024]]),
                            in_=bass.AP(w2_d.tensor, e * FF * D + idx * 128 * D,
                                        [[D, 128], [1, D]])), 16)
                    assert w1s[slot].n == 16 * ordn

        def proj_phase(psQ, psV, qks, q_own, k_own, v_tile, wv_sb):
            pt0, ps0, pv0 = p_t.n, p_s.n, p_v.n

            @block.tensor
            def _(t):
                t.wait_ge(f_ye.h, 128)
                t.wait_ge(s_gq.h, CK["wv"])
                for i in range(NT):
                    if i >= 2:
                        t.wait_ge(p_s.h, ps0 + i - 1)
                    tr_wait(t, 1, i)
                    for h in range(2):
                        for dd in range(DCH):
                            inst = t.matmul(psV[i % 2][:, h * 512:(h + 1) * 512],
                                            xn_fm[:, dd * TPC + i * 128:dd * TPC + (i + 1) * 128],
                                            wv_sb[:, dd * DV + h * 512:dd * DV + (h + 1) * 512],
                                            start=(dd == 0), stop=(dd == DCH - 1))
                    p_t.inc(inst)
                for tt in range(4):
                    if tt >= 2:
                        t.wait_ge(p_s.h, ps0 + 16 + tt - 1)
                    tr_wait(t, 1, 4 * tt + 3)
                    for dd in range(DCH):
                        inst = t.matmul(psQ[:, (tt % 2) * 512:(tt % 2) * 512 + 512],
                                        wqk_sb[:, dd * DQK:(dd + 1) * DQK],
                                        xn_fm[:, dd * TPC + tt * 512:dd * TPC + (tt + 1) * 512],
                                        start=(dd == 0), stop=(dd == DCH - 1))
                    p_t.inc(inst)

            @block.vector
            def _(v):
                v.wait_ge(s_sy.h, CK["smalls"])
                for i in range(NT):
                    v.wait_ge(p_t.h, pt0 + i + 1)
                    p_v.inc(v.tensor_tensor(psV[i % 2][:], psV[i % 2][:], bv_rep[:], ALU.add))
                v.tensor_scalar(gamv[:, 0:1], gam_raw[:, 0:1], CST, None, ALU.mult)
                v.tensor_scalar(gamv[:, 1:2], gam_raw[:, 1:2], CST, None, ALU.mult)
                vs(v.tensor_copy(gamv[:, 2:4], gam_raw[:, 2:4]))
                for tt in range(4):
                    v.wait_ge(p_s.h, ps0 + 16 + tt + 1)
                    v.tensor_scalar(q_own[:, tt * 512:(tt + 1) * 512], qks[:, tt * 512:(tt + 1) * 512],
                                    gamv[:, 0:1], gamv[:, 1:2], ALU.mult, ALU.add)
                    p_v.inc(v.tensor_scalar(k_own[:, tt * 512:(tt + 1) * 512], qks[:, tt * 512:(tt + 1) * 512],
                                            gamv[:, 2:3], gamv[:, 3:4], ALU.mult, ALU.add))
            CK["qk_done"] = p_v.n

            @block.scalar
            def _(s):
                s.wait_ge(s_sy.h, CK["smalls"])
                scq_base = [scq2[0].n, scq2[1].n]
                for i in range(NT):
                    if i >= 2:
                        s.wait_ge(scq2[i % 2].h, scq_base[i % 2] + 16 * (i // 2))
                    s.wait_ge(p_v.h, pv0 + i + 1)
                    s_sf.inc(s.activation(v_tile[:, (i % 2) * DV:(i % 2 + 1) * DV],
                                          psV[i % 2][:], AF.Silu))
                    s.wait_ge(s_sf.h, s_sf.n)
                    p_s.inc(s.nop())
                    scq2[i % 2].inc(s.dma_start(
                        out=bass.AP(vin_dram, i * 128 * DV, [[DV, 128], [1, DV]]),
                        in_=v_tile[:, (i % 2) * DV:(i % 2 + 1) * DV]), 16)
                for tt in range(4):
                    s.wait_ge(p_t.h, pt0 + 16 + tt + 1)
                    p_s.inc(s.activation(qks[:, tt * 512:(tt + 1) * 512],
                                         psQ[:, (tt % 2) * 512:(tt % 2) * 512 + 512],
                                         AF.Silu, bias=bqk_c[:]))
            CK["vin0"] = scq2[0].n
            CK["vin1"] = scq2[1].n
            CK["proj_ps_end"] = p_s.n

            @block.sync
            def _(sy):
                sy.wait_ge(p_v.h, CK["qk_done"])
                s_sy.inc(sy.dma_start(out=kin_dram.ap(), in_=k_own[:]), 16)
                CK["kin"] = s_sy.n
                s_sy.inc(sy.dma_start(out=q_dram.ap(), in_=q_own[:]), 16)
                CK["qdram"] = s_sy.n

            @block.gpsimd
            def _(g):
                g.wait_ge(scq2[0].h, CK["vin0"])
                g.wait_ge(scq2[1].h, CK["vin1"])
                a_cc.inc(g.collective_compute(
                    "AllGather", ALU.bypass, replica_groups=[[0, 1, 2, 3], [4, 5, 6, 7]],
                    ins=[vin_dram.ap()], outs=[vall_dram.ap()]))
                g.wait_ge(s_sy.h, CK["kin"])
                a_cc.inc(g.collective_compute(
                    "AllGather", ALU.bypass, replica_groups=[[0, 1, 2, 3], [4, 5, 6, 7]],
                    ins=[kin_dram.ap()], outs=[kall_dram.ap()]))


        def _unit_bounds(plan):
            last = {}
            for u in plan:
                if u["kind"] == "off":
                    last[u["uo"]] = u["gsub"]
            return last

        def attn_tensor(plan, k_own, q_own, kst, qst, simps, po, attn_sb, v_buf):
            pairs = [(plan[pa], plan[pa + 1] if pa + 1 < len(plan) else None)
                     for pa in range(0, len(plan), 2)]

            def sim_of(u):
                i, w = u["gsub"], u["w"]
                if u["kind"] == "off" and u["first"] and u["qt"] == 0:
                    TT.wait_ge(skq2[u["uo"] % 2].h, 32 * (u["uo"] // 2 + 1))
                if u["kind"] == "diag":
                    ksrc = k_own[:, u["kloc"]:u["kloc"] + 128]
                    qap = q_own[:, u["slot"] * 1024 + u["qoff"]:u["slot"] * 1024 + u["qoff"] + w]
                else:
                    pr = u["uo"] % 2
                    ksrc = kst[pr][:, u["sub"] * 128:(u["sub"] + 1) * 128]
                    qap = qst[pr][:, u["qoff"]:u["qoff"] + w]
                return TT.matmul(simps[i % 2][:, 0:w], ksrc, qap, start=True, stop=True)

            def sims_of_pair(k):
                u0, u1 = pairs[k]
                i = u0["gsub"]
                if i >= 2:
                    TT.wait_ge(a_arg.h, i)
                inst = sim_of(u0)
                n = 1
                if u1 is not None:
                    inst = sim_of(u1)
                    n = 2
                a_sim.inc(inst, n)

            def av_of(u):
                i = u["gsub"]
                TT.wait_ge(avl3[i % 5].h, 16 * (i // 5 + 1))
                for qs in range(u["nqs"]):
                    if u["first"] and u["qtg"] >= 1:
                        g = u["qtg"]
                        TT.wait_ge(a_ev.h, _EVCUM[g - 1] + min(qs + 1, _NQS_SEQ[g - 1]))
                    for h in range(2):
                        inst = TT.matmul(po[qs][:, h * 512:(h + 1) * 512],
                                         attn_sb[:, (i % 4) * 384 + qs * 128:(i % 4) * 384 + qs * 128 + 128],
                                         v_buf[i % 5][:, h * 512:(h + 1) * 512],
                                         start=u["first"], stop=u["last"])
                a_av.inc(inst)

            if pairs[0][0]["kind"] == "diag":
                TT.wait_ge(p_v.h, CK["qk_done"])
                TT.wait_ge(p_s.h, CK["proj_ps_end"])
            sims_of_pair(0)
            for k in range(len(pairs)):
                if k + 1 < len(pairs):
                    sims_of_pair(k + 1)
                u0, u1 = pairs[k]
                i = u0["gsub"]
                TT.wait_ge(a_at.h, i + 1)
                av_of(u0)
                if u1 is not None:
                    TT.wait_ge(a_at.h, i + 2)
                    av_of(u1)

        def attn_vector(plan, z_dstage, z_ustage, simps, po, attn_sb, arg_buf, erf_buf, ftab_sb):
            pairs = [(plan[pa], plan[pa + 1] if pa + 1 < len(plan) else None)
                     for pa in range(0, len(plan), 2)]
            first = True

            def zadd_of(u):
                i, w = u["gsub"], u["w"]
                if u["kind"] == "diag":
                    zap = z_dstage[:, u["zcol"]:u["zcol"] + w]
                else:
                    if u["first"] and u["qt"] == 0:
                        VV.wait_ge(szu2[u["uo"] % 2].h, 48 * (u["uo"] // 2 + 1))
                    zo = 1536 + u["qoff"] - u["sub"] * 128
                    zap = z_ustage[u["uo"] % 2][:, zo:zo + w]
                return VV.tensor_tensor(arg_buf[:, (i % 4) * 384:(i % 4) * 384 + w],
                                        simps[i % 2][:, 0:w], zap, ALU.add)

            def aff_of(u):
                i, w = u["gsub"], u["w"]
                a_at.inc(VV.tensor_scalar(attn_sb[:, (i % 4) * 384:(i % 4) * 384 + w],
                                          erf_buf[:, (i % 4) * 384:(i % 4) * 384 + w],
                                          0.5, 0.5, ALU.mult, ALU.add))

            def evac_of(u):
                i = u["gsub"]
                VV.wait_ge(a_av.h, i + 1)
                for qs in range(u["nqs"]):
                    tloc = u["qoff"] // 128 + qs
                    if u["kind"] == "diag":
                        xap = xt(u["slot"] * 8 + tloc)
                        inst = VV.tensor_tensor(xap, xap, po[qs][:], ALU.add)
                    else:
                        uo = u["uo"]
                        inst = VV.scalar_tensor_tensor(
                            xt(tloc), po[qs][:], ftab_sb[:, 7 + uo:8 + uo],
                            xt(tloc), ALU.mult, ALU.add)
                        inst = VV.scalar_tensor_tensor(
                            xt(8 + tloc), po[qs][:], ftab_sb[:, uo:uo + 1],
                            xt(8 + tloc), ALU.mult, ALU.add)
                    a_ev.inc(inst)

            pend_prev, pend_new = [], None
            for k in range(len(pairs) + 2):
                if k < len(pairs):
                    u0, u1 = pairs[k]
                    i = u0["gsub"]
                    n2 = 2 if u1 is not None else 1
                    if first:
                        VV.wait_ge(s_zb.h, CK["zdstage"])
                        first = False
                    VV.wait_ge(a_sim.h, i + n2)
                    if i >= 4:
                        VV.wait_ge(a_erf.h, i - 2)
                    inst = zadd_of(u0)
                    if u1 is not None:
                        inst = zadd_of(u1)
                    a_arg.inc(inst, n2)
                if 1 <= k <= len(pairs):
                    v0, v1 = pairs[k - 1]
                    j = v0["gsub"]
                    m2 = 2 if v1 is not None else 1
                    VV.wait_ge(a_erf.h, j + m2)
                    if j >= 4:
                        VV.wait_ge(a_av.h, j - 2)
                    aff_of(v0)
                    if v0["last"]:
                        evac_of(v0)
                    if v1 is not None:
                        aff_of(v1)
                        if v1["last"]:
                            if _DELAY_EVAC:
                                pend_new = v1
                            else:
                                evac_of(v1)
                for u in pend_prev:
                    evac_of(u)
                pend_prev = [pend_new] if pend_new is not None else []
                pend_new = None

        def attn_scalar(plan, arg_buf, erf_buf):
            for pa in range(0, len(plan), 2):
                u0 = plan[pa]
                u1 = plan[pa + 1] if pa + 1 < len(plan) else None
                i = u0["gsub"]
                n2 = 2 if u1 is not None else 1
                SS.wait_ge(a_arg.h, i + n2)
                if i >= 4:
                    SS.wait_ge(a_at.h, i - 2)
                inst = SS.activation(erf_buf[:, (i % 4) * 384:(i % 4) * 384 + u0["w"]],
                                     arg_buf[:, (i % 4) * 384:(i % 4) * 384 + u0["w"]], AF.Erf)
                if u1 is not None:
                    j = i + 1
                    inst = SS.activation(erf_buf[:, (j % 4) * 384:(j % 4) * 384 + u1["w"]],
                                         arg_buf[:, (j % 4) * 384:(j % 4) * 384 + u1["w"]], AF.Erf)
                a_erf.inc(inst, n2)

        def attn_gpsimd(plan, kst, qst, z_ustage, v_buf, itab_sb, q_dram):
            bounds = _unit_bounds(plan)
            first_off = True
            voff_col = [14]

            def stage(uo):
                skq2[uo % 2].inc(GG.indirect_dma_start(
                    kst[uo % 2][:], None,
                    bass.AP(kall_dram, 0, [[1024, 128], [1, 1024]]),
                    bass.IndirectOffsetOnAxis(itab_sb[:, uo:uo + 1], 0)), 16)
                skq2[uo % 2].inc(GG.indirect_dma_start(
                    qst[uo % 2][:], None,
                    bass.AP(q_dram, 0, [[1024, 128], [1, 1024]]),
                    bass.IndirectOffsetOnAxis(itab_sb[:, 7 + uo:8 + uo], 0)), 16)
                for j in range(3):
                    szu2[uo % 2].inc(GG.indirect_dma_start(
                        z_ustage[uo % 2][:, j * 1024:(j + 1) * 1024], None,
                        bass.AP(z_dram, 0, [[1024, 128], [1, 1024]]),
                        bass.IndirectOffsetOnAxis(itab_sb[:, 182 + uo * 3 + j:183 + uo * 3 + j], 0)), 16)

            for u in plan:
                i = u["gsub"]
                if u["kind"] == "off" and first_off:
                    GG.wait_ge(a_cc.h, 2)
                    GG.wait_ge(s_sy.h, CK["atab"])
                    GG.wait_ge(s_sy.h, CK["qdram"])
                    GG.wait_ge(s_zb.h, CK["zdram"])
                    stage(0)
                    first_off = False
                if u["kind"] == "off" and u["first"] and u["qt"] == 0:
                    if paranoid:
                        pass
                    elif u["uo"] + 1 < 7:
                        if u["uo"] >= 1:
                            GG.wait_ge(a_sim.h, bounds[u["uo"] - 1] + 1)
                            GG.wait_ge(a_arg.h, bounds[u["uo"] - 1] + 1)
                        stage(u["uo"] + 1)
                if paranoid and u["kind"] == "off" and u["first"] and u["qt"] == 0 and u["uo"] >= 1:
                    GG.wait_ge(a_sim.h, bounds[u["uo"] - 1] + 1)
                    GG.wait_ge(a_arg.h, bounds[u["uo"] - 1] + 1)
                    GG.wait_ge(a_av.h, bounds[u["uo"] - 1] + 1)
                    stage(u["uo"])
                if paranoid and i >= 1:
                    GG.wait_ge(a_av.h, i)
                elif i >= 5:
                    GG.wait_ge(a_av.h, i - 4)
                if u["kind"] == "diag":
                    GG.wait_ge(scq2[0].h, CK["vin0"])
                    GG.wait_ge(scq2[1].h, CK["vin1"])
                    avl3[i % 5].inc(GG.dma_start(out=v_buf[i % 5][:],
                                                 in_=bass.AP(vin_dram, u["vrow"] * DV, [[DV, 128], [1, DV]])), 16)
                else:
                    avl3[i % 5].inc(GG.indirect_dma_start(
                        v_buf[i % 5][:], None,
                        bass.AP(vall_dram, 0, [[DV, 128], [1, DV]]),
                        bass.IndirectOffsetOnAxis(itab_sb[:, voff_col[0]:voff_col[0] + 1], 0)), 16)
                    voff_col[0] += 1
            if any(u["kind"] == "off" for u in plan):
                assert voff_col[0] == 182

        gate_scope = ExitStack()
        if level >= 3:
            gps0a = gate_scope.enter_context(nc.psum_tensor("gps0a", [128, 512], F32))
            gps0b = gate_scope.enter_context(nc.psum_tensor("gps0b", [128, 512], F32))
        if level >= 2:
            tpa = gate_scope.enter_context(nc.psum_tensor("lntp0a", [128, 1024], BF16))
            tpb = gate_scope.enter_context(nc.psum_tensor("lntp0b", [128, 1024], BF16))
            ln_phase(0, [tpa, tpb])
        if level >= 3:
            gating_phase(0, [gps0a, gps0b])
        gate_scope.close()

        ffn1 = ExitStack()
        if level < 4:
            ffn1.close()
        psA0 = None if level < 4 else ffn1.enter_context(nc.psum_tensor("psA0", [128, 1024], F32))
        if level >= 4:
            psB0 = ffn1.enter_context(nc.psum_tensor("psB0", [128, 1024], F32))
            psY00 = ffn1.enter_context(nc.psum_tensor("psY00", [128, 1024], F32))
            psY01 = ffn1.enter_context(nc.psum_tensor("psY01", [128, 1024], F32))
            w1ring0 = ffn1.enter_context(nc.sbuf_tensor("w1ring0", [128, RING_SLOTS * 1024], BF16))
            silu0 = ffn1.enter_context(nc.sbuf_tensor("silu0", [128, 1024], F32))
            hsc0 = ffn1.enter_context(nc.sbuf_tensor("hsc0", [128, JCH * TPC], BF16))

            @block.gpsimd
            def _(g):
                gp_ffn_weights(g, 0, w1ring0, guard=(s_sy.h, CK["zext_store"]))
            ffn_phase(0, w1ring0, psA0, psB0, [psY00, psY01], silu0, hsc0)
            ffn1.close()

        if level >= 5:
            pass

            attn_sbufs = ExitStack()
            q_own = attn_sbufs.enter_context(nc.sbuf_tensor("q_own", [128, TPC], BF16))
            k_own = attn_sbufs.enter_context(nc.sbuf_tensor("k_own", [128, TPC], BF16))
            if level >= 6:
                z_dstage = attn_sbufs.enter_context(nc.sbuf_tensor("z_dstage", [128, 2048], F32))
                zu0 = attn_sbufs.enter_context(nc.sbuf_tensor("zu0", [128, 3072], F32))
                zu1 = attn_sbufs.enter_context(nc.sbuf_tensor("zu1", [128, 3072], F32))
                kst0 = attn_sbufs.enter_context(nc.sbuf_tensor("kst0", [128, 1024], BF16))
                kst1 = attn_sbufs.enter_context(nc.sbuf_tensor("kst1", [128, 1024], BF16))
                qst0 = attn_sbufs.enter_context(nc.sbuf_tensor("qst0", [128, 1024], BF16))
                qst1 = attn_sbufs.enter_context(nc.sbuf_tensor("qst1", [128, 1024], BF16))
                vb0 = attn_sbufs.enter_context(nc.sbuf_tensor("vb0", [128, DV], BF16))
                vb1 = attn_sbufs.enter_context(nc.sbuf_tensor("vb1", [128, DV], BF16))
                vb2 = attn_sbufs.enter_context(nc.sbuf_tensor("vb2", [128, DV], BF16))
                vb3 = attn_sbufs.enter_context(nc.sbuf_tensor("vb3", [128, DV], BF16))
                vb4 = attn_sbufs.enter_context(nc.sbuf_tensor("vb4", [128, DV], BF16))

            proj_scope = ExitStack()
            qks = proj_scope.enter_context(nc.sbuf_tensor("qks", [128, TPC], F32))
            v_tile = proj_scope.enter_context(nc.sbuf_tensor("v_tile", [128, 2 * DV], BF16))
            wv_sb = proj_scope.enter_context(nc.sbuf_tensor("wv_sb", [128, DCH * DV], BF16))
            psQ = proj_scope.enter_context(nc.psum_tensor("psQ", [128, 1024], F32))
            psV0 = proj_scope.enter_context(nc.psum_tensor("psV0", [128, 1024], F32))
            psV1 = proj_scope.enter_context(nc.psum_tensor("psV1", [128, 1024], F32))
            tp1a = proj_scope.enter_context(nc.psum_tensor("lntp1a", [128, 1024], BF16))
            tp1b = proj_scope.enter_context(nc.psum_tensor("lntp1b", [128, 1024], BF16))
            ln_phase(1, [tp1a, tp1b])

            @block.gpsimd
            def _(g):
                g.wait_ge(f_ym.h, 128)
                s_gq.inc(g.dma_start(out=wv_sb[:].rearrange("p (d m) -> p d m", m=DV),
                                     in_=wv_d.rearrange("(d p) m -> p d m", p=128)), 16)
            CK["wv"] = s_gq.n

            proj_phase(psQ, [psV0, psV1], qks, q_own, k_own, v_tile, wv_sb)
            CK["proj_pt_end"] = p_t.n
            proj_scope.close()

            if level >= 6:
                attn_scope = ExitStack()
                sim0 = attn_scope.enter_context(nc.psum_tensor("sim0", [128, 512], F32))
                sim1 = attn_scope.enter_context(nc.psum_tensor("sim1", [128, 512], F32))
                po0 = attn_scope.enter_context(nc.psum_tensor("po0", [128, 1024], F32))
                po1 = attn_scope.enter_context(nc.psum_tensor("po1", [128, 1024], F32))
                po2 = attn_scope.enter_context(nc.psum_tensor("po2", [128, 1024], F32))
                arg_b = attn_scope.enter_context(nc.sbuf_tensor("arg_b", [128, 4 * 384], F32))
                erf_b = attn_scope.enter_context(nc.sbuf_tensor("erf_b", [128, 4 * 384], F32))
                at_b = attn_scope.enter_context(nc.sbuf_tensor("at_b", [128, 4 * 384], BF16))
                simps, po = [sim0, sim1], [po0, po1, po2]
                v_buf = [vb0, vb1, vb2, vb3, vb4]

                @block.sync
                def _(sy):
                    sy.wait_ge(s_zb.h, CK["zdram"])
                    sy.wait_ge(f_ym.h, 128)
                    s_zb.inc(sy.dma_start(out=z_dstage[:],
                                          in_=bass.AP(z_dram, 0, [[ZROW, 128], [1, 2048]])), 16)
                CK["zdstage"] = s_zb.n

                plan = _attn_plan(0)
                kst, qst = [kst0, kst1], [qst0, qst1]
                z_ustage = [zu0, zu1]

                @block.gpsimd
                def _(g):
                    attn_gpsimd(plan, kst, qst, z_ustage, v_buf, itab_sb, q_dram)

                @block.tensor
                def _(t):
                    attn_tensor(plan, k_own, q_own, kst, qst, simps, po, at_b, v_buf)

                @block.scalar
                def _(s):
                    attn_scalar(plan, arg_b, erf_b)

                @block.vector
                def _(v):
                    v.wait_ge(scq2[0].h, CK["vin0"])
                    v.wait_ge(scq2[1].h, CK["vin1"])
                    attn_vector(plan, z_dstage, z_ustage, simps, po, at_b, arg_b, erf_b, ftab_sb)

                attn_scope.close()
            attn_sbufs.close()

            if level >= 7:
                gate_scope1 = ExitStack()
                gps1a = gate_scope1.enter_context(nc.psum_tensor("gps1a", [128, 512], F32))
                gps1b = gate_scope1.enter_context(nc.psum_tensor("gps1b", [128, 512], F32))
                tp2a = gate_scope1.enter_context(nc.psum_tensor("lntp2a", [128, 1024], BF16))
                tp2b = gate_scope1.enter_context(nc.psum_tensor("lntp2b", [128, 1024], BF16))
                ln_phase(2, [tp2a, tp2b])
                gating_phase(1, [gps1a, gps1b])
                gate_scope1.close()

                ffn2 = ExitStack()
                psA1 = ffn2.enter_context(nc.psum_tensor("psA1", [128, 1024], F32))
                psB1 = ffn2.enter_context(nc.psum_tensor("psB1", [128, 1024], F32))
                psY10 = ffn2.enter_context(nc.psum_tensor("psY10", [128, 1024], F32))
                psY11 = ffn2.enter_context(nc.psum_tensor("psY11", [128, 1024], F32))
                w1ring1 = ffn2.enter_context(nc.sbuf_tensor("w1ring1", [128, RING_SLOTS * 1024], BF16))
                silu1 = ffn2.enter_context(nc.sbuf_tensor("silu1", [128, 1024], F32))
                hsc1 = ffn2.enter_context(nc.sbuf_tensor("hsc1", [128, JCH * TPC], BF16))

                @block.gpsimd
                def _(g):
                    gp_ffn_weights(g, 1, w1ring1, guard=(a_ev.h, N_EV))
                ffn_phase(1, w1ring1, psA1, psB1, [psY10, psY11], silu1, hsc1)
                ffn2.close()

        @block.sync
        def _(sy):
            if level >= 7:
                sy.wait_ge(f_ye.h, 256)
            elif level >= 6:
                sy.wait_ge(a_ev.h, N_EV)
            elif level >= 5:
                sy.wait_ge(a_cc.h, 2)
                sy.wait_ge(p_v.h, p_v.n)
                sy.wait_ge(p_s.h, p_s.n)
            elif level >= 4:
                sy.wait_ge(f_ye.h, 128)
            elif level >= 3:
                sy.wait_ge(c_gv.h, CK["gate0"])
                sy.wait_ge(c_gs.h, c_gs.n)
            elif level >= 2:
                sy.wait_ge(t_te.h, t_te.n)
                sy.wait_ge(c_lnv.h, c_lnv.n)
            elif level >= 1:
                sy.wait_ge(s_sf.h, CK["gwT"])
            sy.wait_ge(s_sy.h, s_sy.n)
            sy.wait_ge(s_gq.h, CK["gq_const"])
            s_out.inc(sy.dma_start(out=out_d.rearrange("(t p) m -> p t m", p=128),
                                   in_=x_tm[:].rearrange("p (t m) -> p t m", m=D)), 16)
            sy.wait_ge(s_out.h, 16)

    return nc


_NC_CACHE = [None, None]


def kernel(L0_attn_x, L0_attn_Wqk, L0_attn_bqk, L0_attn_gamma, L0_attn_beta,
           L0_attn_Wv, L0_attn_bv, L0_attn_rel_emb, L0_ffn_gate_w, L0_ffn_w1, L0_ffn_w2):
    x = np.ascontiguousarray(np.asarray(L0_attn_x, dtype=np.float32))
    cmat, ident = _host_consts()
    if _NC_CACHE[0] is None or _NC_CACHE[1] != (_LEVEL, _PARANOID):
        _NC_CACHE[0] = build(_LEVEL, _PARANOID)
        _NC_CACHE[1] = (_LEVEL, _PARANOID)
    nc = _NC_CACHE[0]

    shared = {
        "wqk": np.ascontiguousarray(np.asarray(L0_attn_Wqk, np.float32)),
        "bqk": np.ascontiguousarray(np.asarray(L0_attn_bqk, np.float32).reshape(DQK, 1)),
        "gam": np.ascontiguousarray(np.asarray(L0_attn_gamma, np.float32)),
        "bet": np.ascontiguousarray(np.asarray(L0_attn_beta, np.float32)),
        "wv": np.ascontiguousarray(np.asarray(L0_attn_Wv, np.float32)),
        "bv": np.ascontiguousarray(np.asarray(L0_attn_bv, np.float32).reshape(1, DV)),
        "rel": np.ascontiguousarray(np.asarray(L0_attn_rel_emb, np.float32)),
        "gw": np.ascontiguousarray(np.asarray(L0_ffn_gate_w, np.float32)),
        "w1": np.ascontiguousarray(np.asarray(L0_ffn_w1, np.float32)),
        "w2": np.ascontiguousarray(np.asarray(L0_ffn_w2, np.float32)),
        "cmat": cmat,
        "ident": ident,
    }
    in_maps = []
    for c in range(8):
        g, r = c // 4, c % 4
        x0c = np.concatenate([x[g, r * 1024:(r + 1) * 1024], x[g, (7 - r) * 1024:(8 - r) * 1024]], axis=0)
        itab, ftab = _attn_tables(r)
        in_maps.append({"x0": np.ascontiguousarray(x0c), "itab": itab, "ftab": ftab, **shared})

    res = run_bass_kernel_spmd(nc, in_maps, core_ids=list(range(8)))

    out = np.empty((B, S, D), np.float32)
    for c in range(8):
        g, r = c // 4, c % 4
        oc = res.results[c]["out"]
        out[g, r * 1024:(r + 1) * 1024] = oc[0:1024]
        out[g, (7 - r) * 1024:(8 - r) * 1024] = oc[1024:2048]
    return out


if __name__ == "__main__":
    import sys as _s
    build(int(_s.argv[1]) if len(_s.argv) > 1 else 7)
    print("build ok")


# version 16
# speedup vs baseline: 1.0089x; 1.0089x over previous
import math
from contextlib import ExitStack

import numpy as np

import concourse.bass as bass
import concourse.mybir as mybir
from concourse.bass_utils import run_bass_kernel_spmd

F32 = mybir.dt.float32
BF16 = mybir.dt.bfloat16
AF = mybir.ActivationFunctionType
ALU = mybir.AluOpType
AX = mybir.AxisListType

B, S, D, DQK, DV, FF, E = 2, 8192, 1024, 128, 1024, 1408, 8
DCH, JCH, NT, TPC = 8, 11, 16, 2048
QT_OFF = [0, 384, 768]
QT_W = [384, 384, 256]
DIAG_SUBS = [3, 6, 8]
EPS = 1e-5
MU = math.sqrt(0.5)
SIGSQ2 = math.sqrt(0.25 * math.pi) * math.sqrt(2.0)
CST = 1.0 / (S * SIGSQ2)
ZS = math.sqrt(128.0) / SIGSQ2
ZSHIFT = -MU / SIGSQ2
NEG = -30000.0
ZEXT = 8832
ZLEN = 8704

_LEVEL = 7
_PARANOID = 0
N_DIAG_SUB = 2 * sum(DIAG_SUBS)
N_SUB = N_DIAG_SUB + 7 * 24
N_QT = 27
_DELAY_EVAC = False
N_EV = 72
_NQS_SEQ = [3, 3, 2] * 9
_EVCUM = [0]
for _n in _NQS_SEQ:
    _EVCUM.append(_EVCUM[-1] + _n)


def _bucket_table():
    NUM_BUCKETS, MAX_DISTANCE = 32, 128
    n = np.arange(S, dtype=np.int64)
    max_exact = NUM_BUCKETS // 2
    nf = np.maximum(n, 1).astype(np.float32)
    val_if_large = max_exact + (
        np.log(nf / max_exact) / math.log(MAX_DISTANCE / max_exact) * (NUM_BUCKETS - max_exact)
    ).astype(np.int32)
    val_if_large = np.minimum(val_if_large, NUM_BUCKETS - 1)
    return np.where(n < max_exact, n, val_if_large).astype(np.int64)


def _host_consts():
    import ml_dtypes
    cmat = np.zeros((33, ZEXT), np.float32)
    buck = _bucket_table()
    d = np.arange(ZEXT) - 640
    valid = (d >= 0) & (d < S)
    cmat[32, :] = np.float32(NEG / ZSHIFT)
    cmat[32, valid] = 1.0
    cols = np.nonzero(valid)[0]
    cmat[buck[d[cols]], cols] = ZS
    ident = np.eye(128).astype(ml_dtypes.bfloat16)
    return cmat, ident


def _owner(g):
    return (g, 0) if g < 4 else (7 - g, 1)


class Trk:
    def __init__(self, ctx, nc, name):
        self.h = ctx.enter_context(nc.semaphore(name))
        self.n = 0

    def inc(self, inst, k=1):
        inst.then_inc(self.h, k)
        self.n += k
        return self.n


def _attn_plan(r):
    plan = []
    gsub, qtg = 0, 0
    for slot in (0, 1):
        for qt in range(3):
            w, ns = QT_W[qt], DIAG_SUBS[qt]
            for sub in range(ns):
                plan.append(dict(
                    gsub=gsub, qtg=qtg, w=w, nqs=w // 128,
                    first=(sub == 0), last=(sub == ns - 1),
                    slot=slot, qoff=QT_OFF[qt], kind="diag",
                    kloc=slot * 1024 + sub * 128,
                    zcol=QT_OFF[qt] - sub * 128 + 512,
                    vrow=slot * 1024 + sub * 128, vdiag=True))
                gsub += 1
            qtg += 1
    if r is None:
        return plan
    uo = 0
    for slot, blk in ((0, r), (1, 7 - r)):
        for kb8 in range(blk):
            for qt in range(3):
                w = QT_W[qt]
                for sub in range(8):
                    plan.append(dict(
                        gsub=gsub, qtg=qtg, w=w, nqs=w // 128,
                        first=(sub == 0), last=(sub == 7),
                        qoff=QT_OFF[qt], kind="off", uo=uo, sub=sub, qt=qt))
                    gsub += 1
                qtg += 1
            uo += 1
    assert gsub == N_SUB and qtg == N_QT and uo == 7
    return plan


ZROW = 9216

RING_SLOTS = 16


def _ffn_sched():
    free_of = {}
    lookup = {}
    ordinal = [0] * RING_SLOTS
    slot_vt = [(-1.0, None)] * RING_SLOTS
    for call in (0, 1):
        for e in range(E):
            seq = [("w1", k) for k in range(4)] + [("w2", j) for j in range(JCH)] + \
                  [("w1", k) for k in range(4, 22)]
            for kind, idx in seq:
                s = min(range(RING_SLOTS), key=lambda i: slot_vt[i][0])
                vt_old, ev = slot_vt[s]
                ordinal[s] += 1
                lookup[(kind, call, e, idx)] = (s, ordinal[s], ev)
                if kind == "w1":
                    uw_last = call * 352 + e * 44 + (idx // 2) * 4 + 3
                    slot_vt[s] = (float(uw_last), ("w1", uw_last + 1, idx % 2))
                else:
                    ce = call * 8 + e
                    slot_vt[s] = (call * 352.0 + (e + 1) * 44 - 0.01, ("w2", ce + 1))
    return lookup


_FFN_SCHED = _ffn_sched()



def _attn_tables(r):
    p = np.arange(128)
    itab = np.zeros((128, 203), np.int32)
    ftab = np.zeros((128, 14), np.float32)
    units = []
    for slot, blk in ((0, r), (1, 7 - r)):
        for kb8 in range(blk):
            units.append((slot, blk, kb8))
    assert len(units) == 7
    vcol = 14
    for uo, (slot, blk, kb8) in enumerate(units):
        ro, so = _owner(kb8)
        itab[:, uo] = (ro * 128 + p) * 2 + so
        itab[:, 7 + uo] = 2 * p + slot
        dk = blk - kb8
        for j in range(3):
            itab[:, 182 + uo * 3 + j] = 9 * p + (dk - 1 + j)
        ftab[:, uo] = 1.0 if slot == 1 else 0.0
        ftab[:, 7 + uo] = 1.0 - ftab[:, uo]
        for qt in range(3):
            for sub in range(8):
                itab[:, vcol] = (ro * 2048 + so * 1024 + sub * 128) + p
                vcol += 1
    assert vcol == 182
    return itab, ftab


def build(level=7, paranoid=0):
    nc = bass.Bass("TRN2", target_bir_lowering=False)

    x0 = nc.dram_tensor("x0", [TPC, D], F32, kind="ExternalInput").ap()
    wqk_d = nc.dram_tensor("wqk", [D, DQK], F32, kind="ExternalInput").ap()
    bqk_d = nc.dram_tensor("bqk", [DQK, 1], F32, kind="ExternalInput").ap()
    gam_d = nc.dram_tensor("gam", [2, DQK], F32, kind="ExternalInput").ap()
    bet_d = nc.dram_tensor("bet", [2, DQK], F32, kind="ExternalInput").ap()
    wv_d = nc.dram_tensor("wv", [D, DV], F32, kind="ExternalInput").ap()
    bv_d = nc.dram_tensor("bv", [1, DV], F32, kind="ExternalInput").ap()
    rel_d = nc.dram_tensor("rel", [32, 1], F32, kind="ExternalInput").ap()
    gw_d = nc.dram_tensor("gw", [E, D], F32, kind="ExternalInput").ap()
    w1_d = nc.dram_tensor("w1", [E, D, 2 * FF], F32, kind="ExternalInput").ap()
    w2_d = nc.dram_tensor("w2", [E, FF, D], F32, kind="ExternalInput").ap()
    cmat_d = nc.dram_tensor("cmat", [33, ZEXT], F32, kind="ExternalInput").ap()
    ident_d = nc.dram_tensor("ident", [128, 128], BF16, kind="ExternalInput").ap()
    itab_d = nc.dram_tensor("itab", [128, 203], mybir.dt.int32, kind="ExternalInput").ap()
    ftab_d = nc.dram_tensor("ftab", [128, 14], F32, kind="ExternalInput").ap()
    out_d = nc.dram_tensor("out", [TPC, D], F32, kind="ExternalOutput").ap()

    zext_dram = nc.dram_tensor("zext_dram", [1, ZEXT], F32)
    kin_dram = nc.dram_tensor("kin_dram", [DQK, TPC], BF16)
    kall_dram = nc.dram_tensor("kall_dram", [4 * DQK, TPC], BF16)
    vin_dram = nc.dram_tensor("vin_dram", [TPC, DV], BF16)
    vall_dram = nc.dram_tensor("vall_dram", [4 * TPC, DV], BF16)
    q_dram = nc.dram_tensor("q_dram", [DQK, TPC], BF16)
    z_dram = nc.dram_tensor("z_dram", [128, ZROW], F32)

    TT, VV, SS, GG, SY = nc.tensor, nc.vector, nc.scalar, nc.gpsimd, nc.sync

    ctx = ExitStack()
    with ctx:
        sb = lambda n_, s_, d_=F32: ctx.enter_context(nc.sbuf_tensor(n_, s_, d_))
        x_tm = sb("x_tm", [128, NT * D])
        xn_fm = sb("xn_fm", [128, DCH * TPC], BF16)
        xn_rot = sb("xn_rot", [128, 2 * D], BF16)
        comb = sb("comb", [128, NT * E])
        lg_all = sb("lg_all", [128, NT * E])
        gex_all = sb("gex_all", [128, NT * E])
        gneg = sb("gneg", [128, NT])
        g_scr = sb("g_scr", [128, 4 * E + 8])
        lnS = sb("lnS", [128, NT])
        lnQ = sb("lnQ", [128, NT])
        lnM = sb("lnM", [128, NT])
        lnV = sb("lnV", [128, NT])
        lnR = sb("lnR", [128, NT])
        lnI = sb("lnI", [128, NT])
        sq_scr = sb("sq_scr", [128, D])
        bqk_c = sb("bqk_c", [128, 1])
        gam_raw = sb("gam_raw", [128, 4])
        gamv = sb("gamv", [128, 4])
        bv_rep = sb("bv_rep", [128, DV])
        gwT = sb("gwT", [128, DCH * E], BF16)
        wqk_sb = sb("wqk_sb", [128, DCH * DQK], BF16)
        g8 = sb("g8", [8, D], BF16)
        zl_sb = sb("zl_sb", [33, 1])
        ident_sb = sb("ident_sb", [128, 128], BF16)
        itab_sb = sb("itab_sb", [128, 203], mybir.dt.int32)
        ftab_sb = sb("ftab_sb", [128, 14])

        s_sy = Trk(ctx, nc, "s_sy")
        s_gq = Trk(ctx, nc, "s_gq")
        s_w1l = Trk(ctx, nc, "s_w1l")
        s_w2l = Trk(ctx, nc, "s_w2l")
        s_scq = Trk(ctx, nc, "s_scq")
        t_tp = Trk(ctx, nc, "t_tp")
        t_te = Trk(ctx, nc, "t_te")
        c_lnv = Trk(ctx, nc, "c_lnv")
        c_lns = Trk(ctx, nc, "c_lns")
        c_gt = Trk(ctx, nc, "c_gt")
        c_gv = Trk(ctx, nc, "c_gv")
        c_gs = Trk(ctx, nc, "c_gs")
        f_g = Trk(ctx, nc, "f_g")
        f_u = Trk(ctx, nc, "f_u")
        f_sg = Trk(ctx, nc, "f_sg")
        f_h = Trk(ctx, nc, "f_h")
        f_ym = Trk(ctx, nc, "f_ym")
        f_ye = Trk(ctx, nc, "f_ye")
        p_t = Trk(ctx, nc, "p_t")
        p_s = Trk(ctx, nc, "p_s")
        p_v = Trk(ctx, nc, "p_v")
        a_cc = Trk(ctx, nc, "a_cc")
        a_sim = Trk(ctx, nc, "a_sim")
        a_arg = Trk(ctx, nc, "a_arg")
        a_erf = Trk(ctx, nc, "a_erf")
        a_at = Trk(ctx, nc, "a_at")
        a_av = Trk(ctx, nc, "a_av")
        a_ev = Trk(ctx, nc, "a_ev")
        a_vl = Trk(ctx, nc, "a_vl")
        s_zb = Trk(ctx, nc, "s_zb")
        skq2 = [Trk(ctx, nc, "skq0"), Trk(ctx, nc, "skq1")]
        szu2 = [Trk(ctx, nc, "szu0"), Trk(ctx, nc, "szu1")]
        avl3 = [Trk(ctx, nc, "avl%d" % _a) for _a in range(5)]
        scq2 = [Trk(ctx, nc, "scq0a"), Trk(ctx, nc, "scq1a")]
        w1s = [Trk(ctx, nc, f"w1s{i}") for i in range(RING_SLOTS)]
        z_ev = Trk(ctx, nc, "z_ev")
        v_sf = Trk(ctx, nc, "v_sf")
        s_sf = Trk(ctx, nc, "s_sf")
        s_out = Trk(ctx, nc, "s_out")

        block = ctx.enter_context(nc.Block())
        CK = {}

        def vs(inst):
            v_sf.inc(inst)
            VV.wait_ge(v_sf.h, v_sf.n)

        def xt(i):
            return x_tm[:, i * D:(i + 1) * D]

        @block.gpsimd
        def _(g):
            s_gq.inc(g.dma_start(out=wqk_sb[:].rearrange("p (d c) -> p d c", c=DQK),
                                 in_=wqk_d.rearrange("(d p) c -> p d c", p=128)), 16)
            s_gq.inc(g.dma_start(out=g8[:], in_=gw_d), 16)
        CK["gq_const"] = s_gq.n

        @block.sync
        def _(sy):
            s_sy.inc(sy.dma_start(out=zl_sb[0:32, :], in_=rel_d), 16)
            CK["rel"] = s_sy.n
            s_sy.inc(sy.dma_start(out=x_tm[:].rearrange("p (t m) -> p t m", m=D),
                                  in_=x0.rearrange("(t p) m -> p t m", p=128)), 16)
            CK["x0"] = s_sy.n
            s_sy.inc(sy.dma_start(out=bqk_c[:], in_=bqk_d), 16)
            s_sy.inc(sy.dma_start(out=gam_raw[:, 0:1], in_=gam_d[0:1, :].rearrange("o p -> p o")), 16)
            s_sy.inc(sy.dma_start(out=gam_raw[:, 1:2], in_=bet_d[0:1, :].rearrange("o p -> p o")), 16)
            s_sy.inc(sy.dma_start(out=gam_raw[:, 2:3], in_=gam_d[1:2, :].rearrange("o p -> p o")), 16)
            s_sy.inc(sy.dma_start(out=gam_raw[:, 3:4], in_=bet_d[1:2, :].rearrange("o p -> p o")), 16)
            s_sy.inc(sy.dma_start(out=bv_rep[:], in_=bass.AP(bv_d.tensor, 0, [[0, 128], [1, DV]])), 16)
            s_sy.inc(sy.dma_start(out=itab_sb[:], in_=itab_d), 16)
            s_sy.inc(sy.dma_start(out=ftab_sb[:], in_=ftab_d), 16)
            CK["smalls"] = s_sy.n
            CK["atab"] = s_sy.n

        init_scope = ExitStack()
        zps = init_scope.enter_context(nc.psum_tensor("zps", [128, 1024], F32))
        gtp = init_scope.enter_context(nc.psum_tensor("gtp", [128, DCH * E], BF16))
        cmat_sb = init_scope.enter_context(nc.sbuf_tensor("cmat_sb", [33, ZEXT], F32))
        zext_sb = init_scope.enter_context(nc.sbuf_tensor("zext_sb", [1, ZEXT], F32))

        @block.sync
        def _(sy):
            s_sy.inc(sy.dma_start(out=cmat_sb[:], in_=cmat_d), 16)
            s_sy.inc(sy.dma_start(out=ident_sb[:], in_=ident_d), 16)
        CK["cmat"] = s_sy.n

        @block.vector
        def _(v):
            v.wait_ge(s_sy.h, CK["smalls"])
            s_zb.inc(v.memset(zl_sb[32:33, :], ZSHIFT))
        CK["zl"] = s_zb.n

        zchunks = []
        c0 = 0
        while c0 < ZEXT:
            zchunks.append((c0, min(512, ZEXT - c0)))
            c0 += 512

        @block.tensor
        def _(t):
            t.wait_ge(s_sy.h, CK["cmat"])
            t.wait_ge(s_zb.h, CK["zl"])
            for i, (c, w) in enumerate(zchunks):
                if i >= 2:
                    t.wait_ge(z_ev.h, i - 1)
                s_zb.inc(t.matmul(zps[0:1, (i % 2) * 512:(i % 2) * 512 + w],
                                  zl_sb[:, :], cmat_sb[:, c:c + w], start=True, stop=True))
            t.wait_ge(s_gq.h, CK["gq_const"])
            for dd in range(DCH):
                s_zb.inc(t.transpose(gtp[:, dd * E:(dd + 1) * E],
                                     g8[0:8, dd * 128:(dd + 1) * 128], ident_sb[0:8, 0:8]))
        CK["gtp"] = s_zb.n

        @block.vector
        def _(v):
            for i, (c, w) in enumerate(zchunks):
                v.wait_ge(s_zb.h, CK["zl"] + i + 1)
                z_ev.inc(v.tensor_copy(zext_sb[0:1, c:c + w], zps[0:1, (i % 2) * 512:(i % 2) * 512 + w]))
        CK["zev"] = z_ev.n

        @block.scalar
        def _(s):
            s.wait_ge(s_zb.h, CK["gtp"])
            s_sf.inc(s.copy(gwT[:], gtp[:]))
            s.wait_ge(s_sf.h, s_sf.n)
        CK["gwT"] = s_sf.n

        zpad_sb = init_scope.enter_context(nc.sbuf_tensor("zpad_sb", [128, ZROW - ZLEN], F32))

        @block.vector
        def _(v):
            s_zb.inc(v.memset(zpad_sb[:], 0.0))
        CK["zpad"] = s_zb.n

        @block.sync
        def _(sy):
            sy.wait_ge(z_ev.h, CK["zev"])
            s_sy.inc(sy.dma_start(out=zext_dram.ap(), in_=zext_sb[:]), 16)
            CK["zext_store"] = s_sy.n
            sy.wait_ge(s_sy.h, CK["zext_store"])
            for p in range(128):
                s_zb.inc(sy.dma_start(
                    out=bass.AP(z_dram, p * ZROW, [[1, 1], [1, ZLEN]]),
                    in_=bass.AP(zext_dram, 128 - p, [[1, 1], [1, ZLEN]])), 16)
            sy.wait_ge(s_zb.h, CK["zpad"])
            s_zb.inc(sy.dma_start(
                out=bass.AP(z_dram, ZLEN, [[ZROW, 128], [1, ZROW - ZLEN]]),
                in_=zpad_sb[:]), 16)
        CK["zdram"] = s_zb.n

        init_scope.close()

        def ln_phase(L, tp_ps):
            te0, tp0 = t_te.n, t_tp.n
            for i in range(NT):
                CK[f"tr{L}_{i}"] = te0 + i + 1

            @block.vector
            def _(v):
                if L == 0:
                    v.wait_ge(s_sy.h, CK["x0"])
                for i in range(NT):
                    v.reduce_sum(lnS[:, i:i + 1], xt(i), axis=AX.X)
                    inst = v.scalar_tensor_tensor(sq_scr[:], xt(i), 1.0, xt(i),
                                                  ALU.mult, ALU.mult, accum_out=lnQ[:, i:i + 1])
                    if i == NT - 1:
                        vs(inst)
                v.tensor_scalar_mul(lnM[:], lnS[:], 1.0 / D)
                vs(v.tensor_scalar_mul(lnQ[:], lnQ[:], 1.0 / D))
                vs(v.tensor_tensor(lnV[:], lnM[:], lnM[:], ALU.mult))
                vs(v.tensor_tensor(lnV[:], lnV[:], lnQ[:], ALU.subtract))
                vs(v.tensor_scalar(lnV[:], lnV[:], -1.0, EPS, ALU.mult, ALU.add))
                c_lnv.inc(v.reciprocal(lnR[:], lnV[:]))
                CK[f"ln{L}_rcp"] = c_lnv.n
                v.wait_ge(c_lns.h, L + 1)
                for i in range(NT):
                    if i >= 2:
                        v.wait_ge(t_tp.h, tp0 + i - 1)
                    c_lnv.inc(v.tensor_scalar(xn_rot[:, (i % 2) * D:(i % 2 + 1) * D],
                                              xt(i), lnM[:, i:i + 1], lnI[:, i:i + 1],
                                              ALU.subtract, ALU.mult))
                    CK[f"ln{L}_n{i}"] = c_lnv.n

            @block.tensor
            def _(t):
                if L == 0:
                    t.wait_ge(z_ev.h, CK["zev"])
                    t.wait_ge(s_sf.h, CK["gwT"])
                elif L == 1:
                    t.wait_ge(f_ye.h, 128)
                else:
                    t.wait_ge(a_ev.h, N_EV)
                for i in range(NT):
                    t.wait_ge(c_lnv.h, CK[f"ln{L}_n{i}"])
                    if i >= 2:
                        t.wait_ge(t_te.h, te0 + i - 1)
                    for dd in range(DCH):
                        inst = t.transpose(tp_ps[i % 2][:, dd * 128:(dd + 1) * 128],
                                           xn_rot[:, (i % 2) * D + dd * 128:(i % 2) * D + (dd + 1) * 128],
                                           ident_sb[:])
                    t_tp.inc(inst)

            @block.scalar
            def _(s):
                s.wait_ge(c_lnv.h, CK[f"ln{L}_rcp"])
                c_lns.inc(s.activation(lnI[:], lnR[:], AF.Sqrt))
                for i in range(NT):
                    s.wait_ge(t_tp.h, tp0 + i + 1)
                    for dd in range(DCH):
                        inst = s.copy(xn_fm[:, dd * TPC + i * 128:dd * TPC + (i + 1) * 128],
                                      tp_ps[i % 2][:, dd * 128:(dd + 1) * 128])
                    t_te.inc(inst)

        def tr_wait(eng, L, last_tile):
            eng.wait_ge(t_te.h, CK[f"tr{L}_{last_tile}"])


        def gating_phase(call, gpsAB):
            L = 0 if call == 0 else 2
            gt0, gv0, gs0 = c_gt.n, c_gv.n, c_gs.n

            def gps(i):
                return gpsAB[i % 2][:, 0:8]

            @block.tensor
            def _(t):
                if call == 0:
                    t.wait_ge(s_sf.h, CK["gwT"])
                    t.wait_ge(z_ev.h, CK["zev"])
                else:
                    t.wait_ge(a_ev.h, N_EV)
                for i in range(NT):
                    if i >= 2:
                        t.wait_ge(c_gv.h, gv0 + i - 1)
                    tr_wait(t, L, i)
                    for dd in range(DCH):
                        inst = t.matmul(gps(i),
                                        xn_fm[:, dd * TPC + i * 128:dd * TPC + (i + 1) * 128],
                                        gwT[:, dd * E:(dd + 1) * E],
                                        start=(dd == 0), stop=(dd == DCH - 1))
                    c_gt.inc(inst)

            @block.vector
            def _(v):
                for i in range(NT):
                    v.wait_ge(c_gt.h, gt0 + i + 1)
                    v.reduce_max(gneg[:, i:i + 1], gps(i), axis=AX.X, negate=True)
                    c_gv.inc(v.tensor_copy(lg_all[:, i * 8:(i + 1) * 8], gps(i)))
                CK[f"gatecopies{call}"] = c_gv.n

            @block.scalar
            def _(s):
                for i in range(NT):
                    s.wait_ge(c_gv.h, gv0 + i + 1)
                    c_gs.inc(s.activation(gex_all[:, i * 8:(i + 1) * 8], lg_all[:, i * 8:(i + 1) * 8],
                                          AF.Exp, bias=gneg[:, i:i + 1], scale=1.0))

            eq1 = g_scr[:, 0:E]
            sc2 = g_scr[:, E:2 * E]
            eq2 = g_scr[:, 2 * E:3 * E]
            gtmp = g_scr[:, 3 * E:4 * E]
            t1 = g_scr[:, 4 * E + 0:4 * E + 1]
            t2 = g_scr[:, 4 * E + 1:4 * E + 2]
            sm = g_scr[:, 4 * E + 2:4 * E + 3]
            dn = g_scr[:, 4 * E + 3:4 * E + 4]
            rd = g_scr[:, 4 * E + 4:4 * E + 5]
            cc1 = g_scr[:, 4 * E + 5:4 * E + 6]
            cc2 = g_scr[:, 4 * E + 6:4 * E + 7]

            @block.vector
            def _(v):
                for i in range(NT):
                    ex = gex_all[:, i * 8:(i + 1) * 8]
                    v.wait_ge(c_gs.h, gs0 + i + 1)
                    v.reduce_max(t1, ex, axis=AX.X)
                    vs(v.reduce_sum(sm, ex, axis=AX.X))
                    vs(v.tensor_scalar(eq1, ex, t1, None, ALU.is_ge))
                    vs(v.scalar_tensor_tensor(sc2, eq1, -1e30, ex, ALU.mult, ALU.add))
                    vs(v.reduce_max(t2, sc2, axis=AX.X))
                    v.tensor_scalar(eq2, sc2, t2, None, ALU.is_ge)
                    vs(v.tensor_tensor(dn, t1, t2, ALU.add))
                    vs(v.scalar_tensor_tensor(dn, sm, 1e-20, dn, ALU.mult, ALU.add))
                    vs(v.reciprocal(rd, dn))
                    v.tensor_scalar(cc1, t1, rd, 0.5, ALU.mult, ALU.mult)
                    vs(v.tensor_scalar(cc2, t2, rd, 0.5, ALU.mult, ALU.mult))
                    vs(v.tensor_scalar_mul(gtmp, eq1, cc1))
                    inst = v.scalar_tensor_tensor(comb[:, i * 8:(i + 1) * 8], eq2, cc2, gtmp,
                                                  ALU.mult, ALU.add)
                    if i == NT - 1:
                        c_gv.inc(inst)
            CK[f"gate{call}"] = c_gv.n

        def ffn_phase(call, w1ring, psA, psB, psY, silu_buf, hsc):
            L = 0 if call == 0 else 2

            @block.tensor
            def _(t):
                t.wait_ge(c_gv.h, CK[f"gatecopies{call}"])
                tr_wait(t, L, NT - 1)
                for e in range(E):
                    for j in range(JCH):
                        kg, ku = 2 * j, 2 * j + 1
                        sg, og, _ = _FFN_SCHED[("w1", call, e, kg)]
                        su, ou, _ = _FFN_SCHED[("w1", call, e, ku)]
                        for tt in range(4):
                            uw = call * 352 + e * 44 + j * 4 + tt
                            if uw >= 2:
                                t.wait_ge(f_sg.h, uw - 1)
                            t.wait_ge(w1s[sg].h, 16 * og)
                            tr_wait(t, L, 4 * tt + 3)
                            for dd in range(DCH):
                                inst = t.matmul(
                                    psA[:, (uw % 2) * 512:(uw % 2) * 512 + 512],
                                    w1ring[:, sg * 1024 + dd * 128:sg * 1024 + (dd + 1) * 128],
                                    xn_fm[:, dd * TPC + tt * 512:dd * TPC + (tt + 1) * 512],
                                    start=(dd == 0), stop=(dd == DCH - 1))
                            f_g.inc(inst)
                            if uw >= 2:
                                t.wait_ge(f_h.h, uw - 1)
                            t.wait_ge(w1s[su].h, 16 * ou)
                            for dd in range(DCH):
                                inst = t.matmul(
                                    psB[:, (uw % 2) * 512:(uw % 2) * 512 + 512],
                                    w1ring[:, su * 1024 + dd * 128:su * 1024 + (dd + 1) * 128],
                                    xn_fm[:, dd * TPC + tt * 512:dd * TPC + (tt + 1) * 512],
                                    start=(dd == 0), stop=(dd == DCH - 1))
                            f_u.inc(inst)
                    t.wait_ge(f_h.h, call * 352 + (e + 1) * 44)
                    w2slot = {}
                    for j in range(JCH):
                        s2, o2, _ = _FFN_SCHED[("w2", call, e, j)]
                        w2slot[j] = s2
                        t.wait_ge(w1s[s2].h, 16 * o2)
                    for tt16 in range(NT):
                        uy = call * 128 + e * 16 + tt16
                        if uy >= 2:
                            t.wait_ge(f_ye.h, uy - 1)
                        for j in range(JCH):
                            for h in range(2):
                                inst = t.matmul(
                                    psY[uy % 2][:, h * 512:(h + 1) * 512],
                                    hsc[:, j * TPC + tt16 * 128:j * TPC + (tt16 + 1) * 128],
                                    w1ring[:, w2slot[j] * 1024 + h * 512:w2slot[j] * 1024 + (h + 1) * 512],
                                    start=(j == 0), stop=(j == JCH - 1))
                        f_ym.inc(inst)

            @block.scalar
            def _(s):
                for e in range(E):
                    for j in range(JCH):
                        for tt in range(4):
                            uw = call * 352 + e * 44 + j * 4 + tt
                            s.wait_ge(f_g.h, uw + 1)
                            if uw >= 2:
                                s.wait_ge(f_h.h, uw - 1)
                            f_sg.inc(s.activation(silu_buf[:, (uw % 2) * 512:(uw % 2) * 512 + 512],
                                                  psA[:, (uw % 2) * 512:(uw % 2) * 512 + 512], AF.Silu))

            @block.vector
            def _(v):
                for e in range(E):
                    for j in range(JCH):
                        for tt in range(4):
                            uw = call * 352 + e * 44 + j * 4 + tt
                            v.wait_ge(f_sg.h, uw + 1)
                            v.wait_ge(f_u.h, uw + 1)
                            f_h.inc(v.tensor_tensor(
                                hsc[:, j * TPC + tt * 512:j * TPC + (tt + 1) * 512],
                                silu_buf[:, (uw % 2) * 512:(uw % 2) * 512 + 512],
                                psB[:, (uw % 2) * 512:(uw % 2) * 512 + 512], ALU.mult))
                    for tt16 in range(NT):
                        uy = call * 128 + e * 16 + tt16
                        v.wait_ge(f_ym.h, uy + 1)
                        f_ye.inc(v.scalar_tensor_tensor(
                            xt(tt16), psY[uy % 2][:],
                            comb[:, tt16 * 8 + e:tt16 * 8 + e + 1],
                            xt(tt16), ALU.mult, ALU.add))

        def gp_ffn_weights(g, call, ring, guard=None):
            if guard is not None:
                g.wait_ge(*guard)
            for e in range(E):
                seq = [("w1", k) for k in range(4)] + [("w2", j) for j in range(JCH)] + \
                      [("w1", k) for k in range(4, 22)]
                for kind, idx in seq:
                    slot, ordn, ev = _FFN_SCHED[(kind, call, e, idx)]
                    if ev is not None:
                        if ev[0] == "w1":
                            g.wait_ge((f_g if ev[2] == 0 else f_u).h, ev[1])
                        else:
                            g.wait_ge(f_ym.h, 16 * ev[1])
                    if kind == "w1":
                        col = (idx // 2) + JCH * (idx % 2)
                        w1s[slot].inc(g.dma_start(
                            out=bass.AP(ring, slot * 1024, [[RING_SLOTS * 1024, 128], [128, DCH], [1, 128]]),
                            in_=bass.AP(w1_d.tensor, e * D * 2 * FF + col * 128,
                                        [[2 * FF, 128], [128 * 2 * FF, DCH], [1, 128]])), 16)
                    else:
                        w1s[slot].inc(g.dma_start(
                            out=bass.AP(ring, slot * 1024, [[RING_SLOTS * 1024, 128], [1, 1024]]),
                            in_=bass.AP(w2_d.tensor, e * FF * D + idx * 128 * D,
                                        [[D, 128], [1, D]])), 16)
                    assert w1s[slot].n == 16 * ordn

        def proj_phase(psQ, psV, qks, q_own, k_own, v_tile, wv_sb):
            pt0, ps0, pv0 = p_t.n, p_s.n, p_v.n

            @block.tensor
            def _(t):
                t.wait_ge(f_ye.h, 128)
                t.wait_ge(s_gq.h, CK["wv"])
                for i in range(NT):
                    if i >= 2:
                        t.wait_ge(p_s.h, ps0 + i - 1)
                    tr_wait(t, 1, i)
                    for h in range(2):
                        for dd in range(DCH):
                            inst = t.matmul(psV[i % 2][:, h * 512:(h + 1) * 512],
                                            xn_fm[:, dd * TPC + i * 128:dd * TPC + (i + 1) * 128],
                                            wv_sb[:, dd * DV + h * 512:dd * DV + (h + 1) * 512],
                                            start=(dd == 0), stop=(dd == DCH - 1))
                    p_t.inc(inst)
                for tt in range(4):
                    if tt >= 2:
                        t.wait_ge(p_s.h, ps0 + 16 + tt - 1)
                    tr_wait(t, 1, 4 * tt + 3)
                    for dd in range(DCH):
                        inst = t.matmul(psQ[:, (tt % 2) * 512:(tt % 2) * 512 + 512],
                                        wqk_sb[:, dd * DQK:(dd + 1) * DQK],
                                        xn_fm[:, dd * TPC + tt * 512:dd * TPC + (tt + 1) * 512],
                                        start=(dd == 0), stop=(dd == DCH - 1))
                    p_t.inc(inst)

            @block.vector
            def _(v):
                v.wait_ge(s_sy.h, CK["smalls"])
                for i in range(NT):
                    v.wait_ge(p_t.h, pt0 + i + 1)
                    p_v.inc(v.tensor_tensor(psV[i % 2][:], psV[i % 2][:], bv_rep[:], ALU.add))
                v.tensor_scalar(gamv[:, 0:1], gam_raw[:, 0:1], CST, None, ALU.mult)
                v.tensor_scalar(gamv[:, 1:2], gam_raw[:, 1:2], CST, None, ALU.mult)
                vs(v.tensor_copy(gamv[:, 2:4], gam_raw[:, 2:4]))
                for tt in range(4):
                    v.wait_ge(p_s.h, ps0 + 16 + tt + 1)
                    v.tensor_scalar(q_own[:, tt * 512:(tt + 1) * 512], qks[:, tt * 512:(tt + 1) * 512],
                                    gamv[:, 0:1], gamv[:, 1:2], ALU.mult, ALU.add)
                    p_v.inc(v.tensor_scalar(k_own[:, tt * 512:(tt + 1) * 512], qks[:, tt * 512:(tt + 1) * 512],
                                            gamv[:, 2:3], gamv[:, 3:4], ALU.mult, ALU.add))
            CK["qk_done"] = p_v.n

            @block.scalar
            def _(s):
                s.wait_ge(s_sy.h, CK["smalls"])
                scq_base = [scq2[0].n, scq2[1].n]
                for i in range(NT):
                    if i >= 2:
                        s.wait_ge(scq2[i % 2].h, scq_base[i % 2] + 16 * (i // 2))
                    s.wait_ge(p_v.h, pv0 + i + 1)
                    s_sf.inc(s.activation(v_tile[:, (i % 2) * DV:(i % 2 + 1) * DV],
                                          psV[i % 2][:], AF.Silu))
                    s.wait_ge(s_sf.h, s_sf.n)
                    p_s.inc(s.nop())
                    scq2[i % 2].inc(s.dma_start(
                        out=bass.AP(vin_dram, i * 128 * DV, [[DV, 128], [1, DV]]),
                        in_=v_tile[:, (i % 2) * DV:(i % 2 + 1) * DV]), 16)
                for tt in range(4):
                    s.wait_ge(p_t.h, pt0 + 16 + tt + 1)
                    p_s.inc(s.activation(qks[:, tt * 512:(tt + 1) * 512],
                                         psQ[:, (tt % 2) * 512:(tt % 2) * 512 + 512],
                                         AF.Silu, bias=bqk_c[:]))
            CK["vin0"] = scq2[0].n
            CK["vin1"] = scq2[1].n
            CK["proj_ps_end"] = p_s.n

            @block.sync
            def _(sy):
                sy.wait_ge(p_v.h, CK["qk_done"])
                s_sy.inc(sy.dma_start(out=kin_dram.ap(), in_=k_own[:]), 16)
                CK["kin"] = s_sy.n
                s_sy.inc(sy.dma_start(out=q_dram.ap(), in_=q_own[:]), 16)
                CK["qdram"] = s_sy.n

            @block.gpsimd
            def _(g):
                g.wait_ge(scq2[0].h, CK["vin0"])
                g.wait_ge(scq2[1].h, CK["vin1"])
                a_cc.inc(g.collective_compute(
                    "AllGather", ALU.bypass, replica_groups=[[0, 1, 2, 3], [4, 5, 6, 7]],
                    ins=[vin_dram.ap()], outs=[vall_dram.ap()]))
                g.wait_ge(s_sy.h, CK["kin"])
                a_cc.inc(g.collective_compute(
                    "AllGather", ALU.bypass, replica_groups=[[0, 1, 2, 3], [4, 5, 6, 7]],
                    ins=[kin_dram.ap()], outs=[kall_dram.ap()]))


        def _unit_bounds(plan):
            last = {}
            for u in plan:
                if u["kind"] == "off":
                    last[u["uo"]] = u["gsub"]
            return last

        def attn_tensor(plan, k_own, q_own, kst, qst, simps, po, attn_sb, v_buf):
            pairs = [(plan[pa], plan[pa + 1] if pa + 1 < len(plan) else None)
                     for pa in range(0, len(plan), 2)]

            def sim_of(u):
                i, w = u["gsub"], u["w"]
                if u["kind"] == "off" and u["first"] and u["qt"] == 0:
                    TT.wait_ge(skq2[u["uo"] % 2].h, 32 * (u["uo"] // 2 + 1))
                if u["kind"] == "diag":
                    ksrc = k_own[:, u["kloc"]:u["kloc"] + 128]
                    qap = q_own[:, u["slot"] * 1024 + u["qoff"]:u["slot"] * 1024 + u["qoff"] + w]
                else:
                    pr = u["uo"] % 2
                    ksrc = kst[pr][:, u["sub"] * 128:(u["sub"] + 1) * 128]
                    qap = qst[pr][:, u["qoff"]:u["qoff"] + w]
                return TT.matmul(simps[i % 2][:, 0:w], ksrc, qap, start=True, stop=True)

            def sims_of_pair(k):
                u0, u1 = pairs[k]
                i = u0["gsub"]
                if i >= 2:
                    TT.wait_ge(a_arg.h, i)
                inst = sim_of(u0)
                n = 1
                if u1 is not None:
                    inst = sim_of(u1)
                    n = 2
                a_sim.inc(inst, n)

            def av_of(u):
                i = u["gsub"]
                TT.wait_ge(avl3[i % 5].h, 16 * (i // 5 + 1))
                for qs in range(u["nqs"]):
                    if u["first"] and u["qtg"] >= 1:
                        g = u["qtg"]
                        TT.wait_ge(a_ev.h, _EVCUM[g - 1] + min(qs + 1, _NQS_SEQ[g - 1]))
                    for h in range(2):
                        inst = TT.matmul(po[qs][:, h * 512:(h + 1) * 512],
                                         attn_sb[:, (i % 4) * 384 + qs * 128:(i % 4) * 384 + qs * 128 + 128],
                                         v_buf[i % 5][:, h * 512:(h + 1) * 512],
                                         start=u["first"], stop=u["last"])
                a_av.inc(inst)

            if pairs[0][0]["kind"] == "diag":
                TT.wait_ge(p_v.h, CK["qk_done"])
                TT.wait_ge(p_s.h, CK["proj_ps_end"])
            sims_of_pair(0)
            for k in range(len(pairs)):
                if k + 1 < len(pairs):
                    sims_of_pair(k + 1)
                u0, u1 = pairs[k]
                i = u0["gsub"]
                TT.wait_ge(a_at.h, i + 1)
                av_of(u0)
                if u1 is not None:
                    TT.wait_ge(a_at.h, i + 2)
                    av_of(u1)

        def attn_vector(plan, z_dstage, z_ustage, simps, po, attn_sb, arg_buf, erf_buf, ftab_sb):
            pairs = [(plan[pa], plan[pa + 1] if pa + 1 < len(plan) else None)
                     for pa in range(0, len(plan), 2)]
            first = True

            def zadd_of(u):
                i, w = u["gsub"], u["w"]
                if u["kind"] == "diag":
                    zap = z_dstage[:, u["zcol"]:u["zcol"] + w]
                else:
                    if u["first"] and u["qt"] == 0:
                        VV.wait_ge(szu2[u["uo"] % 2].h, 48 * (u["uo"] // 2 + 1))
                    zo = 1536 + u["qoff"] - u["sub"] * 128
                    zap = z_ustage[u["uo"] % 2][:, zo:zo + w]
                return VV.tensor_tensor(arg_buf[:, (i % 4) * 384:(i % 4) * 384 + w],
                                        simps[i % 2][:, 0:w], zap, ALU.add)

            def aff_of(u):
                i, w = u["gsub"], u["w"]
                a_at.inc(VV.tensor_scalar(attn_sb[:, (i % 4) * 384:(i % 4) * 384 + w],
                                          erf_buf[:, (i % 4) * 384:(i % 4) * 384 + w],
                                          0.5, 0.5, ALU.mult, ALU.add))

            def evac_of(u):
                i = u["gsub"]
                VV.wait_ge(a_av.h, i + 1)
                for qs in range(u["nqs"]):
                    tloc = u["qoff"] // 128 + qs
                    if u["kind"] == "diag":
                        xap = xt(u["slot"] * 8 + tloc)
                        inst = VV.tensor_tensor(xap, xap, po[qs][:], ALU.add)
                    else:
                        uo = u["uo"]
                        inst = VV.scalar_tensor_tensor(
                            xt(tloc), po[qs][:], ftab_sb[:, 7 + uo:8 + uo],
                            xt(tloc), ALU.mult, ALU.add)
                        inst = VV.scalar_tensor_tensor(
                            xt(8 + tloc), po[qs][:], ftab_sb[:, uo:uo + 1],
                            xt(8 + tloc), ALU.mult, ALU.add)
                    a_ev.inc(inst)

            pend_prev, pend_new = [], None
            for k in range(len(pairs) + 2):
                if k < len(pairs):
                    u0, u1 = pairs[k]
                    i = u0["gsub"]
                    n2 = 2 if u1 is not None else 1
                    if first:
                        VV.wait_ge(s_zb.h, CK["zdstage"])
                        first = False
                    VV.wait_ge(a_sim.h, i + n2)
                    if i >= 4:
                        VV.wait_ge(a_erf.h, i - 2)
                    inst = zadd_of(u0)
                    if u1 is not None:
                        inst = zadd_of(u1)
                    a_arg.inc(inst, n2)
                if 1 <= k <= len(pairs):
                    v0, v1 = pairs[k - 1]
                    j = v0["gsub"]
                    m2 = 2 if v1 is not None else 1
                    VV.wait_ge(a_erf.h, j + m2)
                    if j >= 4:
                        VV.wait_ge(a_av.h, j - 2)
                    aff_of(v0)
                    if v0["last"]:
                        evac_of(v0)
                    if v1 is not None:
                        aff_of(v1)
                        if v1["last"]:
                            if _DELAY_EVAC:
                                pend_new = v1
                            else:
                                evac_of(v1)
                for u in pend_prev:
                    evac_of(u)
                pend_prev = [pend_new] if pend_new is not None else []
                pend_new = None

        def attn_scalar(plan, arg_buf, erf_buf):
            for pa in range(0, len(plan), 2):
                u0 = plan[pa]
                u1 = plan[pa + 1] if pa + 1 < len(plan) else None
                i = u0["gsub"]
                n2 = 2 if u1 is not None else 1
                SS.wait_ge(a_arg.h, i + n2)
                if i >= 4:
                    SS.wait_ge(a_at.h, i - 2)
                inst = SS.activation(erf_buf[:, (i % 4) * 384:(i % 4) * 384 + u0["w"]],
                                     arg_buf[:, (i % 4) * 384:(i % 4) * 384 + u0["w"]], AF.Erf)
                if u1 is not None:
                    j = i + 1
                    inst = SS.activation(erf_buf[:, (j % 4) * 384:(j % 4) * 384 + u1["w"]],
                                         arg_buf[:, (j % 4) * 384:(j % 4) * 384 + u1["w"]], AF.Erf)
                a_erf.inc(inst, n2)

        def attn_gpsimd(plan, kst, qst, z_ustage, v_buf, itab_sb, q_dram):
            bounds = _unit_bounds(plan)
            first_off = True
            voff_col = [14]

            def stage(uo):
                skq2[uo % 2].inc(GG.indirect_dma_start(
                    kst[uo % 2][:], None,
                    bass.AP(kall_dram, 0, [[1024, 128], [1, 1024]]),
                    bass.IndirectOffsetOnAxis(itab_sb[:, uo:uo + 1], 0)), 16)
                skq2[uo % 2].inc(GG.indirect_dma_start(
                    qst[uo % 2][:], None,
                    bass.AP(q_dram, 0, [[1024, 128], [1, 1024]]),
                    bass.IndirectOffsetOnAxis(itab_sb[:, 7 + uo:8 + uo], 0)), 16)
                for j in range(3):
                    szu2[uo % 2].inc(GG.indirect_dma_start(
                        z_ustage[uo % 2][:, j * 1024:(j + 1) * 1024], None,
                        bass.AP(z_dram, 0, [[1024, 128], [1, 1024]]),
                        bass.IndirectOffsetOnAxis(itab_sb[:, 182 + uo * 3 + j:183 + uo * 3 + j], 0)), 16)

            for u in plan:
                i = u["gsub"]
                if u["kind"] == "off" and first_off:
                    GG.wait_ge(a_cc.h, 2)
                    GG.wait_ge(s_sy.h, CK["atab"])
                    GG.wait_ge(s_sy.h, CK["qdram"])
                    GG.wait_ge(s_zb.h, CK["zdram"])
                    stage(0)
                    first_off = False
                if u["kind"] == "off" and u["first"] and u["qt"] == 0:
                    if paranoid:
                        pass
                    elif u["uo"] + 1 < 7:
                        if u["uo"] >= 1:
                            GG.wait_ge(a_sim.h, bounds[u["uo"] - 1] + 1)
                            GG.wait_ge(a_arg.h, bounds[u["uo"] - 1] + 1)
                        stage(u["uo"] + 1)
                if paranoid and u["kind"] == "off" and u["first"] and u["qt"] == 0 and u["uo"] >= 1:
                    GG.wait_ge(a_sim.h, bounds[u["uo"] - 1] + 1)
                    GG.wait_ge(a_arg.h, bounds[u["uo"] - 1] + 1)
                    GG.wait_ge(a_av.h, bounds[u["uo"] - 1] + 1)
                    stage(u["uo"])
                if paranoid and i >= 1:
                    GG.wait_ge(a_av.h, i)
                elif i >= 5:
                    GG.wait_ge(a_av.h, i - 4)
                if u["kind"] == "diag":
                    GG.wait_ge(scq2[0].h, CK["vin0"])
                    GG.wait_ge(scq2[1].h, CK["vin1"])
                    avl3[i % 5].inc(GG.dma_start(out=v_buf[i % 5][:],
                                                 in_=bass.AP(vin_dram, u["vrow"] * DV, [[DV, 128], [1, DV]])), 16)
                else:
                    avl3[i % 5].inc(GG.indirect_dma_start(
                        v_buf[i % 5][:], None,
                        bass.AP(vall_dram, 0, [[DV, 128], [1, DV]]),
                        bass.IndirectOffsetOnAxis(itab_sb[:, voff_col[0]:voff_col[0] + 1], 0)), 16)
                    voff_col[0] += 1
            if any(u["kind"] == "off" for u in plan):
                assert voff_col[0] == 182

        gate_scope = ExitStack()
        if level >= 3:
            gps0a = gate_scope.enter_context(nc.psum_tensor("gps0a", [128, 512], F32))
            gps0b = gate_scope.enter_context(nc.psum_tensor("gps0b", [128, 512], F32))
        if level >= 2:
            tpa = gate_scope.enter_context(nc.psum_tensor("lntp0a", [128, 1024], BF16))
            tpb = gate_scope.enter_context(nc.psum_tensor("lntp0b", [128, 1024], BF16))
            ln_phase(0, [tpa, tpb])
        if level >= 3:
            gating_phase(0, [gps0a, gps0b])
        gate_scope.close()

        ffn1 = ExitStack()
        if level < 4:
            ffn1.close()
        psA0 = None if level < 4 else ffn1.enter_context(nc.psum_tensor("psA0", [128, 1024], F32))
        if level >= 4:
            psB0 = ffn1.enter_context(nc.psum_tensor("psB0", [128, 1024], F32))
            psY00 = ffn1.enter_context(nc.psum_tensor("psY00", [128, 1024], F32))
            psY01 = ffn1.enter_context(nc.psum_tensor("psY01", [128, 1024], F32))
            w1ring0 = ffn1.enter_context(nc.sbuf_tensor("w1ring0", [128, RING_SLOTS * 1024], BF16))
            silu0 = ffn1.enter_context(nc.sbuf_tensor("silu0", [128, 1024], F32))
            hsc0 = ffn1.enter_context(nc.sbuf_tensor("hsc0", [128, JCH * TPC], BF16))

            @block.gpsimd
            def _(g):
                gp_ffn_weights(g, 0, w1ring0, guard=(s_sy.h, CK["zext_store"]))
            ffn_phase(0, w1ring0, psA0, psB0, [psY00, psY01], silu0, hsc0)
            ffn1.close()

        if level >= 5:
            pass

            attn_sbufs = ExitStack()
            q_own = attn_sbufs.enter_context(nc.sbuf_tensor("q_own", [128, TPC], BF16))
            k_own = attn_sbufs.enter_context(nc.sbuf_tensor("k_own", [128, TPC], BF16))
            if level >= 6:
                z_dstage = attn_sbufs.enter_context(nc.sbuf_tensor("z_dstage", [128, 2048], F32))
                zu0 = attn_sbufs.enter_context(nc.sbuf_tensor("zu0", [128, 3072], F32))
                zu1 = attn_sbufs.enter_context(nc.sbuf_tensor("zu1", [128, 3072], F32))
                kst0 = attn_sbufs.enter_context(nc.sbuf_tensor("kst0", [128, 1024], BF16))
                kst1 = attn_sbufs.enter_context(nc.sbuf_tensor("kst1", [128, 1024], BF16))
                qst0 = attn_sbufs.enter_context(nc.sbuf_tensor("qst0", [128, 1024], BF16))
                qst1 = attn_sbufs.enter_context(nc.sbuf_tensor("qst1", [128, 1024], BF16))
                vb0 = attn_sbufs.enter_context(nc.sbuf_tensor("vb0", [128, DV], BF16))
                vb1 = attn_sbufs.enter_context(nc.sbuf_tensor("vb1", [128, DV], BF16))
                vb2 = attn_sbufs.enter_context(nc.sbuf_tensor("vb2", [128, DV], BF16))
                vb3 = attn_sbufs.enter_context(nc.sbuf_tensor("vb3", [128, DV], BF16))
                vb4 = attn_sbufs.enter_context(nc.sbuf_tensor("vb4", [128, DV], BF16))

            proj_scope = ExitStack()
            qks = proj_scope.enter_context(nc.sbuf_tensor("qks", [128, TPC], F32))
            v_tile = proj_scope.enter_context(nc.sbuf_tensor("v_tile", [128, 2 * DV], BF16))
            wv_sb = proj_scope.enter_context(nc.sbuf_tensor("wv_sb", [128, DCH * DV], BF16))
            psQ = proj_scope.enter_context(nc.psum_tensor("psQ", [128, 1024], F32))
            psV0 = proj_scope.enter_context(nc.psum_tensor("psV0", [128, 1024], F32))
            psV1 = proj_scope.enter_context(nc.psum_tensor("psV1", [128, 1024], F32))
            tp1a = proj_scope.enter_context(nc.psum_tensor("lntp1a", [128, 1024], BF16))
            tp1b = proj_scope.enter_context(nc.psum_tensor("lntp1b", [128, 1024], BF16))
            ln_phase(1, [tp1a, tp1b])

            @block.gpsimd
            def _(g):
                g.wait_ge(f_ym.h, 128)
                s_gq.inc(g.dma_start(out=wv_sb[:].rearrange("p (d m) -> p d m", m=DV),
                                     in_=wv_d.rearrange("(d p) m -> p d m", p=128)), 16)
            CK["wv"] = s_gq.n

            proj_phase(psQ, [psV0, psV1], qks, q_own, k_own, v_tile, wv_sb)
            CK["proj_pt_end"] = p_t.n
            proj_scope.close()

            if level >= 6:
                attn_scope = ExitStack()
                sim0 = attn_scope.enter_context(nc.psum_tensor("sim0", [128, 512], F32))
                sim1 = attn_scope.enter_context(nc.psum_tensor("sim1", [128, 512], F32))
                po0 = attn_scope.enter_context(nc.psum_tensor("po0", [128, 1024], F32))
                po1 = attn_scope.enter_context(nc.psum_tensor("po1", [128, 1024], F32))
                po2 = attn_scope.enter_context(nc.psum_tensor("po2", [128, 1024], F32))
                arg_b = attn_scope.enter_context(nc.sbuf_tensor("arg_b", [128, 4 * 384], F32))
                erf_b = attn_scope.enter_context(nc.sbuf_tensor("erf_b", [128, 4 * 384], F32))
                at_b = attn_scope.enter_context(nc.sbuf_tensor("at_b", [128, 4 * 384], BF16))
                simps, po = [sim0, sim1], [po0, po1, po2]
                v_buf = [vb0, vb1, vb2, vb3, vb4]

                @block.sync
                def _(sy):
                    sy.wait_ge(s_zb.h, CK["zdram"])
                    sy.wait_ge(f_ym.h, 128)
                    s_zb.inc(sy.dma_start(out=z_dstage[:],
                                          in_=bass.AP(z_dram, 0, [[ZROW, 128], [1, 2048]])), 16)
                CK["zdstage"] = s_zb.n

                plan = _attn_plan(0)
                kst, qst = [kst0, kst1], [qst0, qst1]
                z_ustage = [zu0, zu1]

                @block.gpsimd
                def _(g):
                    attn_gpsimd(plan, kst, qst, z_ustage, v_buf, itab_sb, q_dram)

                @block.tensor
                def _(t):
                    attn_tensor(plan, k_own, q_own, kst, qst, simps, po, at_b, v_buf)

                @block.scalar
                def _(s):
                    attn_scalar(plan, arg_b, erf_b)

                @block.vector
                def _(v):
                    v.wait_ge(scq2[0].h, CK["vin0"])
                    v.wait_ge(scq2[1].h, CK["vin1"])
                    attn_vector(plan, z_dstage, z_ustage, simps, po, at_b, arg_b, erf_b, ftab_sb)

                attn_scope.close()
            attn_sbufs.close()

            if level >= 7:
                gate_scope1 = ExitStack()
                gps1a = gate_scope1.enter_context(nc.psum_tensor("gps1a", [128, 512], F32))
                gps1b = gate_scope1.enter_context(nc.psum_tensor("gps1b", [128, 512], F32))
                tp2a = gate_scope1.enter_context(nc.psum_tensor("lntp2a", [128, 1024], BF16))
                tp2b = gate_scope1.enter_context(nc.psum_tensor("lntp2b", [128, 1024], BF16))
                ln_phase(2, [tp2a, tp2b])
                gating_phase(1, [gps1a, gps1b])
                gate_scope1.close()

                ffn2 = ExitStack()
                psA1 = ffn2.enter_context(nc.psum_tensor("psA1", [128, 1024], F32))
                psB1 = ffn2.enter_context(nc.psum_tensor("psB1", [128, 1024], F32))
                psY10 = ffn2.enter_context(nc.psum_tensor("psY10", [128, 1024], F32))
                psY11 = ffn2.enter_context(nc.psum_tensor("psY11", [128, 1024], F32))
                w1ring1 = ffn2.enter_context(nc.sbuf_tensor("w1ring1", [128, RING_SLOTS * 1024], BF16))
                silu1 = ffn2.enter_context(nc.sbuf_tensor("silu1", [128, 1024], F32))
                hsc1 = ffn2.enter_context(nc.sbuf_tensor("hsc1", [128, JCH * TPC], BF16))

                @block.gpsimd
                def _(g):
                    gp_ffn_weights(g, 1, w1ring1, guard=(a_ev.h, N_EV))
                ffn_phase(1, w1ring1, psA1, psB1, [psY10, psY11], silu1, hsc1)
                ffn2.close()

        @block.sync
        def _(sy):
            if level >= 7:
                sy.wait_ge(f_ye.h, 256)
            elif level >= 6:
                sy.wait_ge(a_ev.h, N_EV)
            elif level >= 5:
                sy.wait_ge(a_cc.h, 2)
                sy.wait_ge(p_v.h, p_v.n)
                sy.wait_ge(p_s.h, p_s.n)
            elif level >= 4:
                sy.wait_ge(f_ye.h, 128)
            elif level >= 3:
                sy.wait_ge(c_gv.h, CK["gate0"])
                sy.wait_ge(c_gs.h, c_gs.n)
            elif level >= 2:
                sy.wait_ge(t_te.h, t_te.n)
                sy.wait_ge(c_lnv.h, c_lnv.n)
            elif level >= 1:
                sy.wait_ge(s_sf.h, CK["gwT"])
            sy.wait_ge(s_sy.h, s_sy.n)
            sy.wait_ge(s_gq.h, CK["gq_const"])
            s_out.inc(sy.dma_start(out=out_d.rearrange("(t p) m -> p t m", p=128),
                                   in_=x_tm[:].rearrange("p (t m) -> p t m", m=D)), 16)
            sy.wait_ge(s_out.h, 16)

    return nc


_NC_CACHE = [None, None]


def kernel(L0_attn_x, L0_attn_Wqk, L0_attn_bqk, L0_attn_gamma, L0_attn_beta,
           L0_attn_Wv, L0_attn_bv, L0_attn_rel_emb, L0_ffn_gate_w, L0_ffn_w1, L0_ffn_w2):
    x = np.ascontiguousarray(np.asarray(L0_attn_x, dtype=np.float32))
    cmat, ident = _host_consts()
    if _NC_CACHE[0] is None or _NC_CACHE[1] != (_LEVEL, _PARANOID):
        _NC_CACHE[0] = build(_LEVEL, _PARANOID)
        _NC_CACHE[1] = (_LEVEL, _PARANOID)
    nc = _NC_CACHE[0]

    shared = {
        "wqk": np.ascontiguousarray(np.asarray(L0_attn_Wqk, np.float32)),
        "bqk": np.ascontiguousarray(np.asarray(L0_attn_bqk, np.float32).reshape(DQK, 1)),
        "gam": np.ascontiguousarray(np.asarray(L0_attn_gamma, np.float32)),
        "bet": np.ascontiguousarray(np.asarray(L0_attn_beta, np.float32)),
        "wv": np.ascontiguousarray(np.asarray(L0_attn_Wv, np.float32)),
        "bv": np.ascontiguousarray(np.asarray(L0_attn_bv, np.float32).reshape(1, DV)),
        "rel": np.ascontiguousarray(np.asarray(L0_attn_rel_emb, np.float32)),
        "gw": np.ascontiguousarray(np.asarray(L0_ffn_gate_w, np.float32)),
        "w1": np.ascontiguousarray(np.asarray(L0_ffn_w1, np.float32)),
        "w2": np.ascontiguousarray(np.asarray(L0_ffn_w2, np.float32)),
        "cmat": cmat,
        "ident": ident,
    }
    in_maps = []
    for c in range(8):
        g, r = c // 4, c % 4
        x0c = np.concatenate([x[g, r * 1024:(r + 1) * 1024], x[g, (7 - r) * 1024:(8 - r) * 1024]], axis=0)
        itab, ftab = _attn_tables(r)
        in_maps.append({"x0": np.ascontiguousarray(x0c), "itab": itab, "ftab": ftab, **shared})

    res = run_bass_kernel_spmd(nc, in_maps, core_ids=list(range(8)))

    out = np.empty((B, S, D), np.float32)
    for c in range(8):
        g, r = c // 4, c % 4
        oc = res.results[c]["out"]
        out[g, r * 1024:(r + 1) * 1024] = oc[0:1024]
        out[g, (7 - r) * 1024:(8 - r) * 1024] = oc[1024:2048]
    return out


if __name__ == "__main__":
    import sys as _s
    build(int(_s.argv[1]) if len(_s.argv) > 1 else 7)
    print("build ok")
